# version 15 (final)
# baseline (speedup 1.0000x reference)
import numpy as np
import ml_dtypes
import concourse.bass as bass
import concourse.mybir as mybir
from concourse.bass_utils import run_bass_kernel_spmd
from contextlib import ExitStack

F32 = mybir.dt.float32
BF16 = mybir.dt.bfloat16
AF = mybir.ActivationFunctionType
ALU = mybir.AluOpType
AXX = mybir.AxisListType.X

D = 1024
HID = 4096
NE = 8
NSS = 65
Q = 128
MCIRC = 255
CH = 512
NT = 4
NCH = 16
NPASS = NCH + 1
SEQCH = 8
LN_EPS = 1e-5
W2LAG = 3

ENGS = ("tensor", "vector", "scalar", "sync", "gpsimd")


class Plan:
    def __init__(self):
        self.ops = {e: [] for e in ENGS}
        self.cnt = {}
        self.semh = {}
        self.dirty = {"vector": set(), "scalar": set()}

    def sem(self, name, handle):
        self.cnt[name] = 0
        self.semh[name] = handle

    def emit(self, eng, fn):
        self.ops[eng].append(fn)

    def wait(self, eng, sname, val):
        assert val <= self.cnt[sname], \
            f"{eng} waits {sname}>={val} planned={self.cnt[sname]}"
        h = self.semh[sname]
        self.emit(eng, lambda E: E.wait_ge(h, val))

    def op(self, eng, build, reads=(), writes=(), inc=None, n=1, fence=False):
        if eng in self.dirty:
            if any(r in self.dirty[eng] for r in reads):
                self.emit(eng, lambda E: E.drain())
                self.dirty[eng].clear()
            for w in writes:
                self.dirty[eng].add(w)
        if inc is None:
            self.emit(eng, build)
            return None
        h = self.semh[inc]
        self.cnt[inc] += n
        v = self.cnt[inc]
        if fence:
            self.emit(eng, build)
            if eng in self.dirty:
                self.dirty[eng].clear()
            self.emit(eng, lambda E: E.drain().then_inc(h, n))
        else:
            self.emit(eng, lambda E: build(E).then_inc(h, n))
        return v

    def fence(self, eng, inc, n=1):
        h = self.semh[inc]
        self.cnt[inc] += n
        v = self.cnt[inc]
        if eng in self.dirty:
            self.dirty[eng].clear()
        self.emit(eng, lambda E: E.drain().then_inc(h, n))
        return v


def host_consts():
    kk = np.arange(Q)
    ii = np.arange(Q)
    jj = np.arange(Q)
    ang = 2 * np.pi / MCIRC
    FWD_R = np.cos(ang * np.outer(127 + ii, kk)).astype(np.float32)
    FWD_I = (-np.sin(ang * np.outer(127 + ii, kk))).astype(np.float32)
    ak = np.full(Q, 2.0 / MCIRC)
    ak[0] = 1.0 / MCIRC
    INV_R = (ak[:, None] * np.cos(ang * np.outer(kk, 127 + ii))).astype(np.float32)
    INV_I = (-ak[:, None] * np.sin(ang * np.outer(kk, 127 + ii))).astype(np.float32)
    KDF_R = np.cos(ang * np.outer(jj, kk)).astype(np.float32)
    KDF_I = (-np.sin(ang * np.outer(jj, kk))).astype(np.float32)
    REV = np.zeros((128, 128), np.float32)
    REV[127 - np.arange(128), np.arange(128)] = 1.0
    EYEF = np.eye(128, dtype=np.float32)
    return dict(c_fwd_r=FWD_R, c_fwd_i=FWD_I, c_inv_r=INV_R, c_inv_i=INV_I,
                c_kdf_r=KDF_R, c_kdf_i=KDF_I, c_rev=REV, c_eyef=EYEF)


def build_program(stop_stage=None, probe_kb=0):
    nc = bass.Bass("TRN2", target_bir_lowering=False)
    es = ExitStack()

    dx = nc.dram_tensor("xin", [NCH * CH, D], F32, kind="ExternalInput").ap()
    dy = nc.dram_tensor("yout", [NCH * CH, D], F32, kind="ExternalOutput").ap()
    dw1 = nc.dram_tensor("w1", [NE * D, HID], F32, kind="ExternalInput").ap()
    dw2 = nc.dram_tensor("w2", [NE * HID, D], F32, kind="ExternalInput").ap()
    dwg = nc.dram_tensor("wgd", [D, NE], F32, kind="ExternalInput").ap()
    db2 = nc.dram_tensor("b2d", [NE, D], F32, kind="ExternalInput").ap()
    db1 = nc.dram_tensor("b1d", [128, 256], F32, kind="ExternalInput").ap()
    dwin = nc.dram_tensor("winTd", [D, D], F32, kind="ExternalInput").ap()
    dgr = nc.dram_tensor("g_repd", [128, D], F32, kind="ExternalInput").ap()
    dbr = nc.dram_tensor("be_repd", [128, D], F32, kind="ExternalInput").ap()
    dbi = nc.dram_tensor("bi_repd", [128, D], F32, kind="ExternalInput").ap()
    dbg = nc.dram_tensor("bg32d", [128, 32], F32, kind="ExternalInput").ap()
    dctr = nc.dram_tensor("ct_red", [NSS, D], F32, kind="ExternalInput").ap()
    dcti = nc.dram_tensor("ct_imd", [NSS, D], F32, kind="ExternalInput").ap()
    dapr = nc.dram_tensor("apad_red", [NSS, 1], F32, kind="ExternalInput").ap()
    dapi = nc.dram_tensor("apad_imd", [NSS, 1], F32, kind="ExternalInput").ap()
    dbpr = nc.dram_tensor("bpad_red", [NSS, 1], F32, kind="ExternalInput").ap()
    dbpi = nc.dram_tensor("bpad_imd", [NSS, 1], F32, kind="ExternalInput").ap()
    dcon = {k: nc.dram_tensor(k, [128, 128], F32, kind="ExternalInput").ap()
            for k in ("c_fwd_r", "c_fwd_i", "c_inv_r", "c_inv_i",
                      "c_kdf_r", "c_kdf_i", "c_rev", "c_eyef")}

    def sbt(name, shape, dt):
        return es.enter_context(nc.sbuf_tensor(name, list(shape), dt))

    S = [sbt("s0", (128, NT * D), F32), sbt("s1", (128, NT * D), F32)]
    lnT = [sbt("lnT_a", (128, 4096), BF16), sbt("lnT_b", (128, 4096), BF16)]
    hbuf = [sbt("h_a", (128, 16 * 512), BF16), sbt("h_b", (128, 16 * 512), BF16)]
    w1r = sbt("w1r", (128, 2 * 4096), BF16)
    w2r = sbt("w2r", (128, 2 * 4096), BF16)
    winT = sbt("winT", (128, 8 * 512), BF16)
    wg = sbt("wg", (128, 64), BF16)
    b2sb = sbt("b2sb", (8, 1024), BF16)
    combT = sbt("combT", (8, 512), BF16)
    bg32 = sbt("bg32", (128, 32), F32)
    g_rep = sbt("g_rep", (128, D), F32)
    be_rep = sbt("be_rep", (128, D), F32)
    bi_rep = sbt("bi_rep", (128, D), F32)
    b1sb = sbt("b1sb", (128, 256), F32)
    ssm_in = sbt("ssm_in", (128, NT * D), F32)
    sq = sbt("sq", (128, D), F32)
    tmpc = sbt("tmpc", (128, D), F32)
    Yfr = sbt("Yfr", (128, D), F32)
    Kfr = sbt("Kfr", (128, D), F32)
    Kfi = sbt("Kfi", (128, D), F32)
    CON = {k: sbt("s" + k, (128, 128), F32) for k in dcon}
    EYEB = sbt("eyeb", (128, 128), BF16)
    CtR = sbt("CtR", (NSS, D), F32)
    CtI = sbt("CtI", (NSS, D), F32)
    POWR = sbt("POWR", (NSS, 132), F32)
    POWI = sbt("POWI", (NSS, 132), F32)
    PkR = sbt("PkR", (NSS, 128), F32)
    PkI = sbt("PkI", (NSS, 128), F32)
    GAnI = sbt("GAnI", (NSS, 128), F32)
    ER = sbt("ERb", (128, NSS), F32)
    EI = sbt("EIb", (128, NSS), F32)
    PTT = sbt("PTT", (128, NSS), F32)
    ssc = sbt("ssc", (NSS, 8), F32)
    sre = sbt("sre0", (NSS, D), F32)
    sim = sbt("sim0", (NSS, D), F32)
    TR = sbt("TRb", (NSS, D), F32)
    TI = sbt("TIb", (NSS, D), F32)
    lt32 = sq
    konst = sbt("konst", (128, 2), F32)
    stats = sbt("stats", (128, 64), F32)
    gsc = sbt("gsc", (128, 160), F32)
    comb = [sbt("comb_a", (128, 32), F32), sbt("comb_b", (128, 32), F32)]
    lnrow = sbt("lnrow", (128, 16), F32)
    if probe_kb:
        sbt("probe", (128, probe_kb * 256), F32)

    pb = [es.enter_context(nc.psum_tensor("pb%d" % i, [128, 512], F32))
          for i in range(8)]

    P = Plan()
    semnames = ["xq", "wv", "tp", "vp", "sp"] + ["wq%d" % p for p in range(NPASS)]
    for s_ in semnames:
        P.sem(s_, es.enter_context(nc.semaphore(s_)))
    s_ld = es.enter_context(nc.semaphore("s_ld"))
    s_su = es.enter_context(nc.semaphore("s_su"))
    s_s2 = es.enter_context(nc.semaphore("s_s2"))
    s_e1 = es.enter_context(nc.semaphore("s_e1"))
    s_e2 = es.enter_context(nc.semaphore("s_e2"))

    SUMX, SUMSQ, MU, VAR, RSTD = 0, 4, 8, 12, 16
    LG, EX, MSK, T8 = 0, 32, 64, 96
    GM1, GM2, GSM, GRC, GCM = 128, 132, 136, 140, 144

    with nc.Block() as blk:
        @blk.sync
        def _(E):
            n = 0
            for k in dcon:
                E.dma_start(out=CON[k][:], in_=dcon[k]).then_inc(s_ld, 16)
                n += 1
            for dst, src in [(CtR, dctr), (CtI, dcti), (g_rep, dgr),
                             (be_rep, dbr), (bi_rep, dbi), (bg32, dbg),
                             (b1sb, db1)]:
                E.dma_start(out=dst[:], in_=src).then_inc(s_ld, 16)
                n += 1
            E.dma_start(out=POWR[:, 1:2], in_=dapr).then_inc(s_ld, 16)
            E.dma_start(out=POWI[:, 1:2], in_=dapi).then_inc(s_ld, 16)
            E.dma_start(out=ssc[:, 0:1], in_=dbpr).then_inc(s_ld, 16)
            E.dma_start(out=ssc[:, 1:2], in_=dbpi).then_inc(s_ld, 16)
            n += 4
            E.wait_ge(s_ld, 16 * (n + 10))

        @blk.gpsimd
        def _(E):
            for j in range(8):
                E.dma_start(out=wg[:, j * 8:(j + 1) * 8],
                            in_=dwg[j * 128:(j + 1) * 128, :]).then_inc(s_ld, 16)
            E.dma_start(out=b2sb[:], in_=db2).then_inc(s_ld, 16)
            E.dma_start(out=EYEB[:], in_=dcon["c_eyef"]).then_inc(s_ld, 16)

    with nc.Block() as blk:
        @blk.vector
        def _(E):
            E.memset(POWR[:, 0:1], 1.0)
            E.memset(POWI[:, 0:1], 0.0)
            E.tensor_copy(ssc[:, 2:3], POWR[:, 1:2])
            E.tensor_copy(ssc[:, 3:4], POWI[:, 1:2])
            E.drain()
            n = 1
            while n < 128:
                E.drain()
                E.tensor_scalar(PkR[:, 0:n], POWI[:, 0:n], ssc[:, 3:4], None, ALU.mult)
                E.drain()
                E.scalar_tensor_tensor(POWR[:, n:2 * n], POWR[:, 0:n], ssc[:, 2:3],
                                       PkR[:, 0:n], ALU.mult, ALU.subtract)
                E.tensor_scalar(PkR[:, 0:n], POWI[:, 0:n], ssc[:, 2:3], None, ALU.mult)
                E.drain()
                E.scalar_tensor_tensor(POWI[:, n:2 * n], POWR[:, 0:n], ssc[:, 3:4],
                                       PkR[:, 0:n], ALU.mult, ALU.add)
                E.tensor_tensor(ssc[:, 4:5], ssc[:, 2:3], ssc[:, 2:3], ALU.mult)
                E.tensor_tensor(ssc[:, 5:6], ssc[:, 3:4], ssc[:, 3:4], ALU.mult)
                E.tensor_tensor(ssc[:, 6:7], ssc[:, 2:3], ssc[:, 3:4], ALU.mult)
                E.drain()
                E.tensor_tensor(ssc[:, 2:3], ssc[:, 4:5], ssc[:, 5:6], ALU.subtract)
                E.tensor_scalar(ssc[:, 3:4], ssc[:, 6:7], 2.0, None, ALU.mult)
                E.drain()
                n *= 2
            E.drain()
            E.tensor_scalar(ssc[:, 4:5], POWI[:, 127:128], POWI[:, 1:2], None, ALU.mult)
            E.tensor_scalar(ssc[:, 5:6], POWI[:, 127:128], POWR[:, 1:2], None, ALU.mult)
            E.drain()
            E.scalar_tensor_tensor(POWR[:, 128:129], POWR[:, 127:128], POWR[:, 1:2],
                                   ssc[:, 4:5], ALU.mult, ALU.subtract)
            E.scalar_tensor_tensor(POWI[:, 128:129], POWR[:, 127:128], POWI[:, 1:2],
                                   ssc[:, 5:6], ALU.mult, ALU.add)
            E.drain()
            E.tensor_scalar(PkR[:, :], POWI[:, 0:128], ssc[:, 1:2], None, ALU.mult)
            E.drain()
            E.scalar_tensor_tensor(PkR[:, :], POWR[:, 0:128], ssc[:, 0:1],
                                   PkR[:, :], ALU.mult, ALU.subtract)
            E.tensor_scalar(PkI[:, :], POWI[:, 0:128], ssc[:, 0:1], None, ALU.mult)
            E.drain()
            E.scalar_tensor_tensor(PkI[:, :], POWR[:, 0:128], ssc[:, 1:2],
                                   PkI[:, :], ALU.mult, ALU.add)
            E.drain()
            E.tensor_scalar(GAnI[:, :], POWI[:, 1:129], -1.0, None, ALU.mult)
            E.memset(sre[:], 0.0)
            E.memset(sim[:], 0.0)
            E.memset(konst[:, 0:1], LN_EPS)
            E.memset(konst[:, 1:2], -0.5)

    for (src, dst, s_e) in ((PkR, ER, s_e1), (PkI, EI, s_e2)):
        with nc.Block() as blk:
            @blk.tensor
            def _(E, src=src):
                E.transpose(pb[0][:, 0:NSS], src[:, :],
                            CON["c_eyef"][0:NSS, 0:NSS]).then_inc(s_e, 1)

            @blk.vector
            def _(E, s_e=s_e):
                E.wait_ge(s_e, 1)
                E.tensor_copy(PTT[:], pb[0][:, 0:NSS]).then_inc(s_e, 1)

            @blk.tensor
            def _(E, s_e=s_e):
                E.wait_ge(s_e, 2)
                E.matmul(pb[1][:, 0:NSS], CON["c_rev"][:], PTT[:],
                         start=True, stop=True).then_inc(s_e, 1)

            @blk.vector
            def _(E, dst=dst, s_e=s_e):
                E.wait_ge(s_e, 3)
                E.tensor_copy(dst[:], pb[1][:, 0:NSS])

    with nc.Block() as blk:
        @blk.vector
        def _(E):
            E.tensor_scalar(PkI[:, :], PkI[:, :], -1.0, None, ALU.mult)

    with nc.Block() as blk:
        @blk.tensor
        def _(E):
            for h in range(2):
                E.matmul(pb[h][:], PkR[:, :], CtR[:, h * 512:(h + 1) * 512],
                         start=True, stop=False)
                E.matmul(pb[h][:], PkI[:, :], CtI[:, h * 512:(h + 1) * 512],
                         start=False, stop=True).then_inc(s_su, 1)

        @blk.vector
        def _(E):
            E.wait_ge(s_su, 2)
            for h in range(2):
                E.tensor_copy(ssm_in[:, h * 512:(h + 1) * 512], pb[h][:])

    with nc.Block() as blk:
        @blk.tensor
        def _(E):
            for h in range(2):
                E.matmul(pb[h][:], CON["c_kdf_r"][:],
                         ssm_in[:, h * 512:(h + 1) * 512], start=True, stop=True)
                E.matmul(pb[2 + h][:], CON["c_kdf_i"][:],
                         ssm_in[:, h * 512:(h + 1) * 512],
                         start=True, stop=True).then_inc(s_s2, 1)

        @blk.vector
        def _(E):
            E.wait_ge(s_s2, 2)
            for h in range(2):
                E.tensor_copy(Kfr[:, h * 512:(h + 1) * 512], pb[h][:])
                E.tensor_copy(Kfi[:, h * 512:(h + 1) * 512], pb[2 + h][:])

    decR = POWR[:, 128:129]
    decI = POWI[:, 128:129]
    GA_R = POWR[:, 1:129]

    state = {"pb_free": [None] * 8,
             "w1slot": 0, "w2slot": 0, "w1free": [None, None],
             "w2free": [None] * 4, "ltfree0": None, "ltfree1": None,
             "winfree": None,
             "lnT_ready": [0, 0], "comb_ready": [0, 0]}
    members_buf = {}

    def wait_mark(eng, mark):
        if mark is not None:
            P.wait(eng, mark[0], mark[1])

    def s_tile(buf, ts):
        return S[buf][:, ts * D:(ts + 1) * D]

    def emit_ln_stats_ts(src_sb, src_name, ts):
        xt = src_sb[:, ts * D:(ts + 1) * D]
        P.op("vector", lambda E, a=xt, c=ts: E.tensor_reduce(
            stats[:, SUMX + c:SUMX + c + 1], a, AXX, ALU.add),
            reads=(src_name,), writes=("stats",))
        P.op("vector", lambda E, a=xt, c=ts: E.scalar_tensor_tensor(
            sq[:], a, 1.0, a, ALU.mult, ALU.mult,
            accum_out=stats[:, SUMSQ + c:SUMSQ + c + 1]),
            reads=(src_name, "sq"), writes=("sq", "stats"))

    def emit_ln_stats_tail():
        P.op("vector", lambda E: E.tensor_scalar(
            stats[:, MU:MU + 4], stats[:, SUMX:SUMX + 4], 1.0 / D, None, ALU.mult),
            reads=("stats",), writes=("stats",))
        P.op("vector", lambda E: E.tensor_scalar(
            stats[:, VAR:VAR + 4], stats[:, SUMSQ:SUMSQ + 4], 1.0 / D, None, ALU.mult),
            reads=("stats",), writes=("stats",))
        P.op("vector", lambda E: E.tensor_tensor(
            lnrow[:, 0:4], stats[:, MU:MU + 4], stats[:, MU:MU + 4], ALU.mult),
            reads=("stats",), writes=("lnrow",))
        vv = P.op("vector", lambda E: E.tensor_tensor(
            stats[:, VAR:VAR + 4], stats[:, VAR:VAR + 4], lnrow[:, 0:4],
            ALU.subtract),
            reads=("stats", "lnrow"), writes=("stats",), inc="vp", fence=True)
        P.wait("scalar", "vp", vv)
        P.op("scalar", lambda E: E.activation(
            lnrow[:, 4:8], stats[:, VAR:VAR + 4], AF.Ln, bias=konst[:, 0:1]),
            reads=(), writes=("lnrow",))
        ss = P.op("scalar", lambda E: E.activation(
            stats[:, RSTD:RSTD + 4], lnrow[:, 4:8], AF.Exp,
            scale=konst[:, 1:2]),
            reads=("lnrow",), writes=("stats",), inc="sp", fence=True)
        P.wait("vector", "sp", ss)

    def emit_ln_stats(src_sb, src_name):
        for ts in range(NT):
            emit_ln_stats_ts(src_sb, src_name, ts)
        emit_ln_stats_tail()

    def emit_ln_t(src_sb, src_name, dst_sb, dst_name, skip_stats=False):
        if not skip_stats:
            emit_ln_stats(src_sb, src_name)
        dst3 = dst_sb[:].rearrange("p (j t) -> p j t", j=8)
        ltb = [lt32, tmpc]
        ltn = ["sq", "tmpc"]
        ltfree = [state["ltfree0"], state["ltfree1"]]
        tvs = [None] * NT

        def norm(ts):
            pi = ts % 2
            wait_mark("vector", ltfree[pi])
            nv = P.op("vector", lambda E, a=src_sb[:, ts * D:(ts + 1) * D],
                      c=ts, b=ltb[pi]: E.tensor_scalar(
                b[:], a, stats[:, MU + c:MU + c + 1],
                stats[:, RSTD + c:RSTD + c + 1], ALU.subtract, ALU.mult),
                reads=(src_name, "stats", ltn[pi], "tm0", "tm1"),
                writes=(ltn[pi],), inc="vp", fence=True)
            P.wait("tensor", "vp", nv)
            bks = (6, 7) if pi == 0 else (4, 5)
            tv = None
            for g in range(2):
                bk = bks[g]
                wait_mark("tensor", state["pb_free"][bk])
                for q in range(4):
                    dj = g * 4 + q
                    last = (g == 1 and q == 3)
                    tv2 = P.op("tensor",
                        lambda E, b=bk, qq=q, a=dj, lb=ltb[pi]: E.transpose(
                            pb[b][:, qq * 128:(qq + 1) * 128],
                            lb[:, a * 128:(a + 1) * 128], CON["c_eyef"][:]),
                        inc="tp" if last else None)
                    if last:
                        tv = tv2
            ltfree[pi] = ("tp", tv)
            tvs[ts] = tv

        def copies(ts):
            pi = ts % 2
            bks = (6, 7) if pi == 0 else (4, 5)
            P.wait("vector", "tp", tvs[ts])
            for g in range(2):
                bk = bks[g]
                dv = P.op("vector", lambda E, b=bk, gg=g, c=ts: E.tensor_copy(
                    dst3[:, gg * 4:(gg + 1) * 4, c * 128:(c + 1) * 128],
                    pb[b][:].rearrange("p (j t) -> p j t", j=4)),
                    reads=(), writes=(dst_name,), inc="vp", fence=True)
                state["pb_free"][bk] = ("vp", dv)

        norm(0)
        for ts in range(1, NT):
            norm(ts)
            copies(ts - 1)
        copies(NT - 1)
        state["ltfree0"], state["ltfree1"] = ltfree
        return P.cnt["vp"]

    def emit_gates(lnT_sb, comb_sb, comb_name, lnT_ready):
        P.wait("tensor", "vp", lnT_ready)
        wait_mark("tensor", state["pb_free"][4])
        tv = None
        for ts in range(NT):
            for dj in range(8):
                last = (ts == NT - 1 and dj == 7)
                tv2 = P.op("tensor", lambda E, c=ts, a=dj: E.matmul(
                    pb[4][:, c * 8:(c + 1) * 8],
                    lnT_sb[:, a * 512 + c * 128:a * 512 + c * 128 + 128],
                    wg[:, a * 8:(a + 1) * 8],
                    start=(a == 0), stop=(a == 7)),
                    inc="tp" if last else None)
                if last:
                    tv = tv2
        P.wait("vector", "tp", tv)
        P.op("vector", lambda E: E.tensor_tensor(
            gsc[:, LG:LG + 32], pb[4][:, 0:32], bg32[:], ALU.add),
            reads=(), writes=("gsc",))
        state["pb_free"][4] = ("vp", P.fence("vector", "vp"))
        lg3 = gsc[:, LG:LG + 32].rearrange("p (t e) -> p t e", e=8)
        t83 = gsc[:, T8:T8 + 32].rearrange("p (t e) -> p t e", e=8)
        ex3 = gsc[:, EX:EX + 32].rearrange("p (t e) -> p t e", e=8)
        gm1_3 = gsc[:, GM1:GM1 + 4].rearrange("p (t o) -> p t o", o=1)
        gm2_3 = gsc[:, GM2:GM2 + 4].rearrange("p (t o) -> p t o", o=1)
        gsm_3 = gsc[:, GSM:GSM + 4].rearrange("p (t o) -> p t o", o=1)
        P.op("vector", lambda E: E.tensor_reduce(gm1_3, lg3, AXX, ALU.max),
             reads=("gsc",), writes=("gsc",))
        for ts in range(NT):
            P.op("vector", lambda E, c=ts: E.tensor_scalar(
                gsc[:, MSK + c * 8:MSK + (c + 1) * 8],
                gsc[:, LG + c * 8:LG + (c + 1) * 8],
                gsc[:, GM1 + c:GM1 + c + 1], None, ALU.is_ge),
                reads=("gsc",), writes=("gsc",))
        for ts in range(NT):
            P.op("vector", lambda E, c=ts: E.scalar_tensor_tensor(
                gsc[:, T8 + c * 8:T8 + (c + 1) * 8],
                gsc[:, MSK + c * 8:MSK + (c + 1) * 8], -1e30,
                gsc[:, LG + c * 8:LG + (c + 1) * 8], ALU.mult, ALU.add),
                reads=("gsc",), writes=("gsc",))
        P.op("vector", lambda E: E.tensor_reduce(gm2_3, t83, AXX, ALU.max),
             reads=("gsc",), writes=("gsc",))
        for ts in range(NT):
            P.op("vector", lambda E, c=ts: E.tensor_scalar(
                gsc[:, T8 + c * 8:T8 + (c + 1) * 8],
                gsc[:, T8 + c * 8:T8 + (c + 1) * 8],
                gsc[:, GM2 + c:GM2 + c + 1], None, ALU.is_ge),
                reads=("gsc",), writes=("gsc",))
        P.op("vector", lambda E: E.tensor_tensor(
            gsc[:, MSK:MSK + 32], gsc[:, MSK:MSK + 32], gsc[:, T8:T8 + 32],
            ALU.add),
            reads=("gsc",), writes=("gsc",))
        vv = P.fence("vector", "vp")
        P.wait("scalar", "vp", vv)
        ss = P.op("scalar", lambda E: E.activation(
            gsc[:, EX:EX + 32], gsc[:, LG:LG + 32], AF.Exp),
            reads=(), writes=("gsc",), inc="sp", fence=True)
        P.wait("vector", "sp", ss)
        P.op("vector", lambda E: E.tensor_reduce(gsm_3, ex3, AXX, ALU.add),
             reads=("gsc",), writes=("gsc",))
        P.op("vector", lambda E: E.reciprocal(
            gsc[:, GRC:GRC + 4], gsc[:, GSM:GSM + 4]),
            reads=("gsc",), writes=("gsc",))
        for ts in range(NT):
            P.op("vector", lambda E, c=ts: E.tensor_tensor(
                gsc[:, GCM:GCM + 8], gsc[:, EX + c * 8:EX + (c + 1) * 8],
                gsc[:, MSK + c * 8:MSK + (c + 1) * 8], ALU.mult),
                reads=("gsc",), writes=("gsc",))
            P.op("vector", lambda E, c=ts: E.tensor_scalar(
                comb_sb[:, c * 8:(c + 1) * 8], gsc[:, GCM:GCM + 8],
                gsc[:, GRC + c:GRC + c + 1], 0.5, ALU.mult, ALU.mult),
                reads=("gsc",), writes=(comb_name,))
        return P.fence("vector", "vp")

    def emit_b2(comb_sb, comb_name, dst_buf, comb_ready):
        P.wait("tensor", "vp", comb_ready)
        wait_mark("tensor", state["pb_free"][5])
        tv = None
        for ts in range(NT):
            last = (ts == NT - 1)
            tv2 = P.op("tensor", lambda E, c=ts: E.transpose(
                pb[5][0:8, c * 128:(c + 1) * 128],
                comb_sb[:, c * 8:(c + 1) * 8], CON["c_eyef"][:]),
                inc="tp" if last else None)
            if last:
                tv = tv2
        P.wait("vector", "tp", tv)
        cv = P.op("vector", lambda E: E.tensor_copy(combT[:], pb[5][0:8, 0:512]),
                  reads=(), writes=("combT",), inc="vp", fence=True)
        state["pb_free"][5] = ("vp", cv)
        P.wait("tensor", "vp", cv)
        for ts in range(NT):
            for dh in range(2):
                bk = 4 + dh
                wait_mark("tensor", state["pb_free"][bk])
                tv2 = P.op("tensor", lambda E, c=ts, d=dh, b=bk: E.matmul(
                    pb[b][:], combT[:, c * 128:(c + 1) * 128],
                    b2sb[:, d * 512:(d + 1) * 512], start=True, stop=True),
                    inc="tp" if dh == 1 else None)
                if dh == 1:
                    tv = tv2
            P.wait("vector", "tp", tv)
            for dh in range(2):
                bk = 4 + dh
                dst = S[dst_buf][:, ts * D + dh * 512:ts * D + dh * 512 + 512]
                state["pb_free"][bk] = ("vp", P.op(
                    "vector", lambda E, o=dst, b=bk: E.tensor_tensor(
                        o, o, pb[b][:], ALU.add),
                    reads=("s%d" % dst_buf,), writes=("s%d" % dst_buf,),
                    inc="vp", fence=True))

    W1BK = {0: (0, 1, 4), 1: (2, 3, 5)}

    def emit_pass(p, members):
        wq = "wq%d" % p
        for (mi, ck) in members:
            P.wait("tensor", "vp", state["lnT_ready"][mi])
        for e in range(NE):
            for half in range(2):
                for w in range(4):
                    slot = state["w1slot"] % 2
                    state["w1slot"] += 1
                    wait_mark("gpsimd", state["w1free"][slot])
                    for dj in range(8):
                        P.op("gpsimd", lambda E, a=e, b=half, c=w, d=dj, s=slot:
                             E.dma_start(
                                 out=w1r[:, s * 4096 + d * 512:
                                         s * 4096 + d * 512 + 512],
                                 in_=dw1[a * D + d * 128:a * D + (d + 1) * 128,
                                         b * 2048 + c * 512:
                                         b * 2048 + (c + 1) * 512]),
                             inc=wq, n=16)
                    P.wait("tensor", wq, P.cnt[wq])
                    for k in range(4):
                        ht = w * 4 + k
                        for (mi, ck) in members:
                            bk = W1BK[mi][ht % 3]
                            wait_mark("tensor", state["pb_free"][bk])
                        for dj in range(8):
                            for wi, (mi, ck) in enumerate(members):
                                bk = W1BK[mi][ht % 3]
                                lastmm = (dj == 7 and wi == len(members) - 1)
                                tv2 = P.op("tensor",
                                     lambda E, a=dj, b=bk, m=mi, s=slot, kk=k:
                                     E.matmul(
                                         pb[b][:],
                                         w1r[:, s * 4096 + a * 512 + kk * 128:
                                             s * 4096 + a * 512 + kk * 128 + 128],
                                         lnT[m][:, a * 512:(a + 1) * 512],
                                         start=(a == 0), stop=(a == 7)),
                                     inc="tp" if lastmm else None)
                                if lastmm:
                                    tv = tv2
                        P.wait("scalar", "tp", tv)
                        for (mi, ck) in members:
                            bk = W1BK[mi][ht % 3]
                            state["pb_free"][bk] = ("sp", P.op(
                                "scalar",
                                lambda E, b=bk, m=mi, hh=ht, a=e, hf=half:
                                E.activation(
                                    hbuf[m][:, hh * 512:(hh + 1) * 512],
                                    pb[b][:], AF.Gelu,
                                    bias=b1sb[:, a * 32 + hf * 16 + hh:
                                              a * 32 + hf * 16 + hh + 1]),
                                reads=(), writes=("h_%d" % mi,),
                                inc="sp", fence=True))
                    state["w1free"][slot] = ("tp", tv)
                gelu_done = P.cnt["sp"]
                for dh in range(2):
                    hg_wq = [None] * 4
                    for hg in range(4):
                        slot = state["w2slot"] % 4
                        state["w2slot"] += 1
                        wait_mark("gpsimd", state["w2free"][slot])
                        for d_ in range(4):
                            ht = hg * 4 + d_
                            P.op("gpsimd",
                                 lambda E, a=e, b=half, c=dh, d=ht, s=slot:
                                 E.dma_start(
                                     out=w2r[:, s * 2048 + (d % 4) * 512:
                                             s * 2048 + (d % 4) * 512 + 512],
                                     in_=dw2[a * HID + b * 2048 + d * 128:
                                             a * HID + b * 2048 + (d + 1) * 128,
                                             c * 512:(c + 1) * 512]),
                                 inc=wq, n=16)
                        hg_wq[hg] = P.cnt[wq]
                        state["hg_slot%d" % hg] = slot
                    P.wait("tensor", "sp", gelu_done)
                    nmem = len(members)
                    lag = W2LAG if nmem == 2 else 0
                    nsteps = 16 + lag
                    for step in range(nsteps):
                        work = []
                        if step < 16:
                            work.append((0, step))
                        if nmem == 2 and 0 <= step - lag < 16:
                            work.append((1, step - lag))
                        evac_req = []
                        for wi, (midx, ht) in enumerate(work):
                            mi, ck = members[midx]
                            if ht % 4 == 0:
                                P.wait("tensor", wq, hg_wq[ht // 4])
                            slot = state["hg_slot%d" % (ht // 4)]
                            base = [0, 4][midx] if nmem == 2 else 0
                            for ts in range(NT):
                                bk = base + ts
                                if ht == 0:
                                    wait_mark("tensor", state["pb_free"][bk])
                                lastmm = (wi == len(work) - 1 and ts == NT - 1)
                                tv2 = P.op("tensor",
                                     lambda E, b=bk, m=mi, d=ht, c=ts, s=slot:
                                     E.matmul(
                                         pb[b][:],
                                         hbuf[m][:, d * 512 + c * 128:
                                                 d * 512 + c * 128 + 128],
                                         w2r[:, s * 2048 + (d % 4) * 512:
                                             s * 2048 + (d % 4) * 512 + 512],
                                         start=(d == 0), stop=(d == 15)),
                                     inc="tp" if lastmm else None)
                                if lastmm:
                                    tv = tv2
                            if ht == 15:
                                evac_req.append((midx, base))
                        if step - lag in (3, 7, 11) and step >= lag:
                            state["w2free"][state["hg_slot%d" % ((step - lag) // 4)]] = ("tp", tv)
                        for (midx, base) in evac_req:
                            mi, ck = members[midx]
                            P.wait("vector", "tp", tv)
                            sbuf_i = members_buf[mi]
                            for ts in reversed(range(NT)):
                                bk = base + ts
                                dst = S[sbuf_i][:, ts * D + dh * 512:
                                                ts * D + dh * 512 + 512]
                                state["pb_free"][bk] = ("vp", P.op(
                                    "vector",
                                    lambda E, o=dst, b=bk, m=mi, c=ts, a=e:
                                    E.scalar_tensor_tensor(
                                        o, pb[b][:],
                                        comb[m][:, c * 8 + a:c * 8 + a + 1],
                                        o, ALU.mult, ALU.add),
                                    reads=("s%d" % sbuf_i,),
                                    writes=("s%d" % sbuf_i,),
                                    inc="vp", fence=True))
                                if (mi == 1 and e == NE - 1 and half == 1
                                        and dh == 1):
                                    emit_ln_stats_ts(S[sbuf_i],
                                                     "s%d" % sbuf_i, ts)
                                    if ts == 0:
                                        emit_ln_stats_tail()
                    state["w2free"][state["hg_slot3"]] = ("tp", tv)

    def emit_tprod():
        P.op("vector", lambda E: E.tensor_tensor(
            sq[0:NSS, :], CtI[:], sim[:], ALU.mult),
            reads=("sim0", "sq"), writes=("sq",))
        P.op("vector", lambda E: E.tensor_tensor(
            TR[:], CtR[:], sre[:], ALU.mult),
            reads=("sre0",), writes=("TRb",))
        P.op("vector", lambda E: E.tensor_tensor(
            TR[:], TR[:], sq[0:NSS, :], ALU.subtract),
            reads=("TRb", "sq"), writes=("TRb",))
        P.op("vector", lambda E: E.tensor_tensor(
            tmpc[0:NSS, :], CtI[:], sre[:], ALU.mult),
            reads=("sre0", "tmpc", "tm0", "tm1"), writes=("tmpc",))
        P.op("vector", lambda E: E.tensor_tensor(
            TI[:], CtR[:], sim[:], ALU.mult),
            reads=("sim0",), writes=("TIb",))
        P.op("vector", lambda E: E.tensor_tensor(
            TI[:], TI[:], tmpc[0:NSS, :], ALU.add),
            reads=("TIb", "tmpc"), writes=("TIb",))
        return P.fence("vector", "vp")

    def emit_ssm(c):
        buf = c % 2
        if c == SEQCH:
            P.op("vector", lambda E: E.memset(sre[:], 0.0), writes=("sre0",))
            P.op("vector", lambda E: E.memset(sim[:], 0.0), writes=("sim0",))
        tprod0 = emit_tprod()
        ln2v = emit_ln_t(S[buf], "s%d" % buf, lnT[1], "lnT_b",
                         skip_stats=True)
        P.wait("tensor", "vp", ln2v)
        for dh in range(2):
            if dh == 0:
                wait_mark("gpsimd", state["winfree"])
            for j in range(8):
                wv = P.op("gpsimd", lambda E, jj=j, d=dh: E.dma_start(
                    out=winT[:, jj * 512:(jj + 1) * 512],
                    in_=dwin[jj * 128:(jj + 1) * 128, d * 512:(d + 1) * 512]),
                    inc="wv", n=16)
            P.wait("tensor", "wv", wv)
            for ts in range(NT):
                bk = ts % 2
                wait_mark("tensor", state["pb_free"][bk])
                for dj in range(8):
                    tv2 = P.op("tensor", lambda E, a=dj, b=bk, c2=ts: E.matmul(
                        pb[b][:],
                        lnT[1][:, a * 512 + c2 * 128:a * 512 + c2 * 128 + 128],
                        winT[:, a * 512:(a + 1) * 512],
                        start=(a == 0), stop=(a == 7)),
                        inc="tp" if dj == 7 else None)
                    if dj == 7:
                        tv = tv2
                P.wait("vector", "tp", tv)
                state["pb_free"][bk] = ("vp", P.op(
                    "vector", lambda E, c2=ts, d=dh, b=bk: E.tensor_tensor(
                        ssm_in[:, c2 * D + d * 512:c2 * D + d * 512 + 512],
                        pb[b][:], bi_rep[:, d * 512:(d + 1) * 512], ALU.add),
                    reads=(), writes=("ssm_in",), inc="vp", fence=True))
                if dh == 1:
                    emit_ln_stats_ts(ssm_in, "ssm_in", ts)
            if dh == 0:
                P.wait("gpsimd", "tp", tv)
            else:
                state["winfree"] = ("tp", tv)
        emit_ln_stats_tail()
        for ts in range(NT):
            xt = ssm_in[:, ts * D:(ts + 1) * D]
            P.op("vector", lambda E, a=xt, c2=ts: E.tensor_scalar(
                tmpc[:], a, stats[:, MU + c2:MU + c2 + 1],
                stats[:, RSTD + c2:RSTD + c2 + 1], ALU.subtract, ALU.mult),
                reads=("ssm_in", "stats"), writes=("tmpc",))
            P.op("vector", lambda E: E.tensor_tensor(
                tmpc[:], tmpc[:], g_rep[:], ALU.mult),
                reads=("tmpc",), writes=("tmpc",))
            P.op("vector", lambda E, a=xt: E.tensor_tensor(
                a, tmpc[:], be_rep[:], ALU.add),
                reads=("tmpc",), writes=("ssm_in",))
        ssm_ready = P.fence("vector", "vp")
        P.wait("tensor", "vp", ssm_ready)

        def contrib_mms(ts2):
            xin2 = ssm_in[:, ts2 * D:(ts2 + 1) * D]
            for bk in range(4):
                wait_mark("tensor", state["pb_free"][bk])
            cd = None
            for h2 in range(2):
                P.op("tensor", lambda E, h=h2, a=xin2: E.matmul(
                    pb[h][0:NSS, :], ER[:], a[:, h * 512:(h + 1) * 512],
                    start=True, stop=True))
                cd = P.op("tensor", lambda E, h=h2, a=xin2: E.matmul(
                    pb[2 + h][0:NSS, :], EI[:], a[:, h * 512:(h + 1) * 512],
                    start=True, stop=True),
                    inc="tp" if h2 == 1 else None)
            return cd

        ctr_done = contrib_mms(0)
        tprod = tprod0
        for ts in range(NT):
            xin_t = ssm_in[:, ts * D:(ts + 1) * D]
            P.wait("tensor", "vp", tprod)
            for bk in (4, 5):
                wait_mark("tensor", state["pb_free"][bk])
            far_done = None
            for h2 in range(2):
                P.op("tensor", lambda E, h=h2: E.matmul(
                    pb[4 + h][:], GA_R, TR[:, h * 512:(h + 1) * 512],
                    start=True, stop=False))
                far_done = P.op("tensor", lambda E, h=h2: E.matmul(
                    pb[4 + h][:], GAnI[:], TI[:, h * 512:(h + 1) * 512],
                    start=False, stop=False),
                    inc="tp" if h2 == 1 else None)
            P.wait("vector", "tp", ctr_done)
            P.op("vector", lambda E: E.tensor_scalar(
                sq[0:NSS, :], sim[:], decI, None, ALU.mult),
                reads=("sim0", "sq"), writes=("sq",))
            P.op("vector", lambda E: E.tensor_scalar(
                tmpc[0:NSS, :], sre[:], decI, None, ALU.mult),
                reads=("sre0", "tmpc", "tm0", "tm1"), writes=("tmpc",))
            P.op("vector", lambda E: E.scalar_tensor_tensor(
                sre[:], sre[:], decR, sq[0:NSS, :], ALU.mult, ALU.subtract),
                reads=("sre0", "sq"), writes=("sre0",))
            for h2 in range(2):
                P.op("vector", lambda E, h=h2: E.tensor_tensor(
                    sre[:, h * 512:(h + 1) * 512],
                    sre[:, h * 512:(h + 1) * 512], pb[h][0:NSS, :], ALU.add),
                    reads=("sre0",), writes=("sre0",))
            P.op("vector", lambda E: E.scalar_tensor_tensor(
                sim[:], sim[:], decR, tmpc[0:NSS, :], ALU.mult, ALU.add),
                reads=("sim0", "tmpc"), writes=("sim0",))
            for h2 in range(2):
                P.op("vector", lambda E, h=h2: E.tensor_tensor(
                    sim[:, h * 512:(h + 1) * 512],
                    sim[:, h * 512:(h + 1) * 512], pb[2 + h][0:NSS, :],
                    ALU.add),
                    reads=("sim0",), writes=("sim0",))
            st_done = P.fence("vector", "vp")
            if ts < NT - 1:
                P.wait("vector", "tp", far_done)
                tprod = emit_tprod()
            if ts < NT - 1:
                P.wait("tensor", "vp", st_done)
                ctr_done = contrib_mms(ts + 1)
            y_done = None
            for h2 in range(2):
                sl = slice(h2 * 512, (h2 + 1) * 512)
                for bk in (6, 7):
                    wait_mark("tensor", state["pb_free"][bk])
                P.op("tensor", lambda E, a=xin_t, s_=sl: E.matmul(
                    pb[6][:], CON["c_fwd_r"][:], a[:, s_],
                    start=True, stop=True))
                xf_done = P.op("tensor", lambda E, a=xin_t, s_=sl: E.matmul(
                    pb[7][:], CON["c_fwd_i"][:], a[:, s_],
                    start=True, stop=True), inc="tp")
                P.wait("vector", "tp", xf_done)
                P.op("vector", lambda E, s_=sl: E.tensor_tensor(
                    Yfr[:, 0:512], pb[6][:], Kfr[:, s_], ALU.mult),
                    reads=("Yf0",), writes=("Yf0",))
                P.op("vector", lambda E, s_=sl: E.tensor_tensor(
                    tmpc[:, 0:512], pb[7][:], Kfi[:, s_], ALU.mult),
                    reads=("tm0",), writes=("tm0",))
                P.op("vector", lambda E, s_=sl: E.tensor_tensor(
                    Yfr[:, 512:1024], pb[6][:], Kfi[:, s_], ALU.mult),
                    reads=("Yf1",), writes=("Yf1",))
                P.op("vector", lambda E, s_=sl: E.tensor_tensor(
                    tmpc[:, 512:1024], pb[7][:], Kfr[:, s_], ALU.mult),
                    reads=("tm1",), writes=("tm1",))
                P.op("vector", lambda E: E.tensor_tensor(
                    Yfr[:, 0:512], Yfr[:, 0:512], tmpc[:, 0:512], ALU.subtract),
                    reads=("Yf0", "tm0"), writes=("Yf0",))
                P.op("vector", lambda E: E.tensor_tensor(
                    Yfr[:, 512:1024], Yfr[:, 512:1024], tmpc[:, 512:1024],
                    ALU.add),
                    reads=("Yf1", "tm1"), writes=("Yf1",))
                yfh = P.fence("vector", "vp")
                state["pb_free"][6] = ("vp", yfh)
                state["pb_free"][7] = ("vp", yfh)
                P.wait("tensor", "vp", yfh)
                P.op("tensor", lambda E, h=h2: E.matmul(
                    pb[4 + h][:], CON["c_inv_r"][:],
                    Yfr[:, 0:512], start=False, stop=False))
                y_done = P.op("tensor", lambda E, h=h2: E.matmul(
                    pb[4 + h][:], CON["c_inv_i"][:],
                    Yfr[:, 512:1024], start=False, stop=True),
                    inc="tp")
            P.wait("vector", "tp", y_done)
            for h2 in range(2):
                dst = S[buf][:, ts * D + h2 * 512:ts * D + h2 * 512 + 512]
                P.op("vector", lambda E, h=h2, a=xin_t: E.tensor_tensor(
                    tmpc[:, h * 512:(h + 1) * 512], pb[4 + h][:],
                    a[:, h * 512:(h + 1) * 512], ALU.add),
                    reads=("ssm_in", "tmpc", "tm0", "tm1"), writes=("tmpc",)),
                P.op("vector", lambda E, o=dst, h=h2: E.tensor_tensor(
                    o, o, tmpc[:, h * 512:(h + 1) * 512], ALU.add),
                    reads=("tmpc", "s%d" % buf), writes=("s%d" % buf,))
            emit_ln_stats_ts(S[buf], "s%d" % buf, ts)
            if ts == NT - 1:
                emit_ln_stats_tail()
            ev = P.fence("vector", "vp")
            state["pb_free"][4] = ("vp", ev)
            state["pb_free"][5] = ("vp", ev)

    xv = dx.rearrange("(c t p) d -> c p t d", t=NT, p=128)
    yv = dy.rearrange("(c t p) d -> c p t d", t=NT, p=128)

    def emit_load(c):
        sv = S[c % 2][:].rearrange("p (t d) -> p t d", t=NT)
        return P.op("sync", lambda E, cc=c, o=sv: E.dma_start(out=o, in_=xv[cc]),
                    inc="xq", n=16)

    def emit_store(c, v_ready):
        P.wait("sync", "vp", v_ready)
        sv = S[c % 2][:].rearrange("p (t d) -> p t d", t=NT)
        return P.op("sync", lambda E, cc=c, i=sv: E.dma_start(out=yv[cc], in_=i),
                    inc="xq", n=16)

    stage_n = [0]

    class StopSched(Exception):
        pass

    def tick():
        stage_n[0] += 1
        if stop_stage is not None and stage_n[0] >= stop_stage:
            raise StopSched()

    l0 = emit_load(0)
    l1 = emit_load(1)
    P.wait("vector", "xq", l0)
    state["lnT_ready"][1] = emit_ln_t(S[0], "s0", lnT[1], "lnT_b")
    cv = emit_gates(lnT[1], comb[1], "comb_b", state["lnT_ready"][1])
    emit_b2(comb[1], "comb_b", 0, cv)

    store_after = {}
    try:
      tick()
      for p in range(NPASS):
        members = []
        if p >= 1:
            members.append((0, p - 1))
        if p <= NCH - 1:
            members.append((1, p))
        members_buf.clear()
        for (mi, ck) in members:
            members_buf[mi] = ck % 2
        emit_pass(p, members)
        tick()
        if p >= 1:
            store_after[p - 1] = P.cnt["vp"]
        if p <= NCH - 1:
            c = p
            if p >= 1:
                stv = emit_store(p - 1, store_after[p - 1])
                if p + 1 <= NCH - 1:
                    P.wait("sync", "xq", stv)
                    emit_load(p + 1)
            emit_ssm(c)
            tick()
            state["lnT_ready"][0] = emit_ln_t(
                S[c % 2], "s%d" % (c % 2), lnT[0], "lnT_a", skip_stats=True)
            cva = emit_gates(lnT[0], comb[0], "comb_a", state["lnT_ready"][0])
            emit_b2(comb[0], "comb_a", c % 2, cva)
            tick()
            if p + 1 <= NCH - 1:
                P.wait("vector", "xq", P.cnt["xq"])
                state["lnT_ready"][1] = emit_ln_t(
                    S[(p + 1) % 2], "s%d" % ((p + 1) % 2), lnT[1], "lnT_b")
                cvb = emit_gates(lnT[1], comb[1], "comb_b",
                                 state["lnT_ready"][1])
                emit_b2(comb[1], "comb_b", (p + 1) % 2, cvb)
                tick()
      emit_store(NCH - 1, store_after[NCH - 1])
    except StopSched:
        fv = P.fence("vector", "vp")
        emit_store(0, fv)
    P.wait("sync", "xq", P.cnt["xq"])

    with nc.Block() as blk:
        @blk.tensor
        def _(E):
            for f in P.ops["tensor"]:
                f(E)

        @blk.vector
        def _(E):
            for f in P.ops["vector"]:
                f(E)

        @blk.scalar
        def _(E):
            for f in P.ops["scalar"]:
                f(E)

        @blk.sync
        def _(E):
            for f in P.ops["sync"]:
                f(E)

        @blk.gpsimd
        def _(E):
            for f in P.ops["gpsimd"]:
                f(E)

    es.close()
    return nc


_NC_CACHE = None


def kernel(L0_attn_x, L0_attn_W_in, L0_attn_b_in, L0_attn_gamma, L0_attn_beta,
           L0_attn_A_re, L0_attn_A_im, L0_attn_B_re, L0_attn_B_im,
           L0_attn_C_re, L0_attn_C_im, L0_attn_D_re, L0_attn_D_im,
           L0_ffn_Wg, L0_ffn_bg, L0_ffn_W1, L0_ffn_b1, L0_ffn_W2, L0_ffn_b2):
    global _NC_CACHE
    x = np.asarray(L0_attn_x, dtype=np.float32)
    B, L, _ = x.shape
    shared = host_consts()
    shared["w1"] = np.ascontiguousarray(
        np.asarray(L0_ffn_W1, np.float32).reshape(NE * D, HID))
    shared["w2"] = np.ascontiguousarray(
        np.asarray(L0_ffn_W2, np.float32).reshape(NE * HID, D))
    shared["wgd"] = np.ascontiguousarray(np.asarray(L0_ffn_Wg, np.float32))
    shared["b2d"] = np.ascontiguousarray(np.asarray(L0_ffn_b2, np.float32))
    b1 = np.asarray(L0_ffn_b1, np.float32)
    shared["b1d"] = np.ascontiguousarray(
        b1.reshape(NE, 32, 128).transpose(2, 0, 1).reshape(128, 256))
    shared["winTd"] = np.ascontiguousarray(np.asarray(L0_attn_W_in, np.float32).T)
    shared["g_repd"] = np.ascontiguousarray(np.broadcast_to(
        np.asarray(L0_attn_gamma, np.float32)[None, :], (128, D)))
    shared["be_repd"] = np.ascontiguousarray(np.broadcast_to(
        np.asarray(L0_attn_beta, np.float32)[None, :], (128, D)))
    shared["bi_repd"] = np.ascontiguousarray(np.broadcast_to(
        np.asarray(L0_attn_b_in, np.float32)[None, :], (128, D)))
    shared["bg32d"] = np.ascontiguousarray(np.broadcast_to(
        np.tile(np.asarray(L0_ffn_bg, np.float32), 4)[None, :], (128, 32)))
    shared["ct_red"] = np.ascontiguousarray(np.concatenate(
        [np.asarray(L0_attn_C_re, np.float32).T,
         np.asarray(L0_attn_D_re, np.float32)[None, :]], axis=0))
    shared["ct_imd"] = np.ascontiguousarray(np.concatenate(
        [np.asarray(L0_attn_C_im, np.float32).T,
         np.zeros((1, D), np.float32)], axis=0))
    shared["apad_red"] = np.ascontiguousarray(np.concatenate(
        [np.asarray(L0_attn_A_re, np.float32), [np.float32(1.0)]])[:, None])
    shared["apad_imd"] = np.ascontiguousarray(np.concatenate(
        [np.asarray(L0_attn_A_im, np.float32), [np.float32(0.0)]])[:, None])
    shared["bpad_red"] = np.ascontiguousarray(np.concatenate(
        [np.asarray(L0_attn_B_re, np.float32), [np.float32(1.0)]])[:, None])
    shared["bpad_imd"] = np.ascontiguousarray(np.concatenate(
        [np.asarray(L0_attn_B_im, np.float32), [np.float32(0.0)]])[:, None])

    if _NC_CACHE is None:
        _NC_CACHE = build_program()
    nc = _NC_CACHE

    in_maps = []
    for i in range(8):
        m = dict(shared)
        m["xin"] = np.ascontiguousarray(x[2 * i:2 * i + 2].reshape(NCH * CH, D))
        in_maps.append(m)

    res = run_bass_kernel_spmd(nc, in_maps, core_ids=list(range(8)))
    out = np.stack([np.asarray(r["yout"]).reshape(2, L, D) for r in res.results])
    return out.reshape(B, L, D).astype(np.float32)


# version 16
# speedup vs baseline: 1.1364x; 1.1364x over previous
import numpy as np
import ml_dtypes
import concourse.bass as bass
import concourse.mybir as mybir
from concourse.bass_utils import run_bass_kernel_spmd
from contextlib import ExitStack

F32 = mybir.dt.float32
BF16 = mybir.dt.bfloat16
AF = mybir.ActivationFunctionType
ALU = mybir.AluOpType
AXX = mybir.AxisListType.X

D = 1024
HID = 4096
NE = 8
NSS = 65
Q = 128
MCIRC = 255
CH = 512
NT = 4
NCH = 16
NPASS = NCH + 1
SEQCH = 8
LN_EPS = 1e-5
W2LAG = 3

ENGS = ("tensor", "vector", "scalar", "sync", "gpsimd")


class Plan:
    def __init__(self):
        self.ops = {e: [] for e in ENGS}
        self.cnt = {}
        self.semh = {}
        self.dirty = {"vector": set(), "scalar": set()}

    def sem(self, name, handle):
        self.cnt[name] = 0
        self.semh[name] = handle

    def emit(self, eng, fn):
        self.ops[eng].append(fn)

    def wait(self, eng, sname, val):
        assert val <= self.cnt[sname], \
            f"{eng} waits {sname}>={val} planned={self.cnt[sname]}"
        h = self.semh[sname]
        self.emit(eng, lambda E: E.wait_ge(h, val))

    def op(self, eng, build, reads=(), writes=(), inc=None, n=1, fence=False):
        if eng in self.dirty:
            if any(r in self.dirty[eng] for r in reads):
                self.emit(eng, lambda E: E.drain())
                self.dirty[eng].clear()
            for w in writes:
                self.dirty[eng].add(w)
        if inc is None:
            self.emit(eng, build)
            return None
        h = self.semh[inc]
        self.cnt[inc] += n
        v = self.cnt[inc]
        if fence:
            self.emit(eng, build)
            if eng in self.dirty:
                self.dirty[eng].clear()
            self.emit(eng, lambda E: E.drain().then_inc(h, n))
        else:
            self.emit(eng, lambda E: build(E).then_inc(h, n))
        return v

    def fence(self, eng, inc, n=1):
        h = self.semh[inc]
        self.cnt[inc] += n
        v = self.cnt[inc]
        if eng in self.dirty:
            self.dirty[eng].clear()
        self.emit(eng, lambda E: E.drain().then_inc(h, n))
        return v


def host_consts():
    kk = np.arange(Q)
    ii = np.arange(Q)
    jj = np.arange(Q)
    ang = 2 * np.pi / MCIRC
    FWD_R = np.cos(ang * np.outer(127 + ii, kk)).astype(np.float32)
    FWD_I = (-np.sin(ang * np.outer(127 + ii, kk))).astype(np.float32)
    ak = np.full(Q, 2.0 / MCIRC)
    ak[0] = 1.0 / MCIRC
    INV_R = (ak[:, None] * np.cos(ang * np.outer(kk, 127 + ii))).astype(np.float32)
    INV_I = (-ak[:, None] * np.sin(ang * np.outer(kk, 127 + ii))).astype(np.float32)
    KDF_R = np.cos(ang * np.outer(jj, kk)).astype(np.float32)
    KDF_I = (-np.sin(ang * np.outer(jj, kk))).astype(np.float32)
    REV = np.zeros((128, 128), np.float32)
    REV[127 - np.arange(128), np.arange(128)] = 1.0
    EYEF = np.eye(128, dtype=np.float32)
    return dict(c_fwd_r=FWD_R, c_fwd_i=FWD_I, c_inv_r=INV_R, c_inv_i=INV_I,
                c_kdf_r=KDF_R, c_kdf_i=KDF_I, c_rev=REV, c_eyef=EYEF)


def build_program(stop_stage=None, probe_kb=0):
    nc = bass.Bass("TRN2", target_bir_lowering=False)
    es = ExitStack()

    dx = nc.dram_tensor("xin", [NCH * CH, D], F32, kind="ExternalInput").ap()
    dy = nc.dram_tensor("yout", [NCH * CH, D], F32, kind="ExternalOutput").ap()
    dw1 = nc.dram_tensor("w1", [NE * D, HID], F32, kind="ExternalInput").ap()
    dw2 = nc.dram_tensor("w2", [NE * HID, D], F32, kind="ExternalInput").ap()
    dwg = nc.dram_tensor("wgd", [D, NE], F32, kind="ExternalInput").ap()
    db2 = nc.dram_tensor("b2d", [NE, D], F32, kind="ExternalInput").ap()
    db1 = nc.dram_tensor("b1d", [128, 256], F32, kind="ExternalInput").ap()
    dwin = nc.dram_tensor("winTd", [D, D], F32, kind="ExternalInput").ap()
    dgr = nc.dram_tensor("g_repd", [128, D], F32, kind="ExternalInput").ap()
    dbr = nc.dram_tensor("be_repd", [128, D], F32, kind="ExternalInput").ap()
    dbi = nc.dram_tensor("bi_repd", [128, D], F32, kind="ExternalInput").ap()
    dbg = nc.dram_tensor("bg32d", [128, 32], F32, kind="ExternalInput").ap()
    dctr = nc.dram_tensor("ct_red", [NSS, D], F32, kind="ExternalInput").ap()
    dcti = nc.dram_tensor("ct_imd", [NSS, D], F32, kind="ExternalInput").ap()
    dapr = nc.dram_tensor("apad_red", [NSS, 1], F32, kind="ExternalInput").ap()
    dapi = nc.dram_tensor("apad_imd", [NSS, 1], F32, kind="ExternalInput").ap()
    dbpr = nc.dram_tensor("bpad_red", [NSS, 1], F32, kind="ExternalInput").ap()
    dbpi = nc.dram_tensor("bpad_imd", [NSS, 1], F32, kind="ExternalInput").ap()
    dcon = {k: nc.dram_tensor(k, [128, 128], F32, kind="ExternalInput").ap()
            for k in ("c_fwd_r", "c_fwd_i", "c_inv_r", "c_inv_i",
                      "c_kdf_r", "c_kdf_i", "c_rev", "c_eyef")}

    def sbt(name, shape, dt):
        return es.enter_context(nc.sbuf_tensor(name, list(shape), dt))

    S = [sbt("s0", (128, NT * D), F32), sbt("s1", (128, NT * D), F32)]
    lnT = [sbt("lnT_a", (128, 4096), BF16), sbt("lnT_b", (128, 4096), BF16)]
    hbuf = [sbt("h_a", (128, 16 * 512), BF16), sbt("h_b", (128, 16 * 512), BF16)]
    w1r = sbt("w1r", (128, 2 * 4096), BF16)
    w2r = sbt("w2r", (128, 2 * 4096), BF16)
    winT = sbt("winT", (128, 8 * 512), BF16)
    wg = sbt("wg", (128, 64), BF16)
    b2sb = sbt("b2sb", (8, 1024), BF16)
    combT = sbt("combT", (8, 512), BF16)
    bg32 = sbt("bg32", (128, 32), F32)
    g_rep = sbt("g_rep", (128, D), F32)
    be_rep = sbt("be_rep", (128, D), F32)
    bi_rep = sbt("bi_rep", (128, D), F32)
    b1sb = sbt("b1sb", (128, 256), F32)
    ssm_in = sbt("ssm_in", (128, NT * D), F32)
    sq = sbt("sq", (128, D), F32)
    tmpc = sbt("tmpc", (128, D), F32)
    Yfr = sbt("Yfr", (128, D), F32)
    Kfr = sbt("Kfr", (128, D), F32)
    Kfi = sbt("Kfi", (128, D), F32)
    CON = {k: sbt("s" + k, (128, 128), F32) for k in dcon}
    EYEB = sbt("eyeb", (128, 128), BF16)
    CtR = sbt("CtR", (NSS, D), F32)
    CtI = sbt("CtI", (NSS, D), F32)
    POWR = sbt("POWR", (NSS, 132), F32)
    POWI = sbt("POWI", (NSS, 132), F32)
    PkR = sbt("PkR", (NSS, 128), F32)
    PkI = sbt("PkI", (NSS, 128), F32)
    GAnI = sbt("GAnI", (NSS, 128), F32)
    ER = sbt("ERb", (128, NSS), F32)
    EI = sbt("EIb", (128, NSS), F32)
    PTT = sbt("PTT", (128, NSS), F32)
    ssc = sbt("ssc", (NSS, 8), F32)
    sre = sbt("sre0", (NSS, D), F32)
    sim = sbt("sim0", (NSS, D), F32)
    TR = sbt("TRb", (NSS, D), F32)
    TI = sbt("TIb", (NSS, D), F32)
    lt32 = sq
    konst = sbt("konst", (128, 2), F32)
    stats = sbt("stats", (128, 64), F32)
    gsc = sbt("gsc", (128, 160), F32)
    comb = [sbt("comb_a", (128, 32), F32), sbt("comb_b", (128, 32), F32)]
    lnrow = sbt("lnrow", (128, 16), F32)
    if probe_kb:
        sbt("probe", (128, probe_kb * 256), F32)

    pb = [es.enter_context(nc.psum_tensor("pb%d" % i, [128, 512], F32))
          for i in range(8)]

    P = Plan()
    semnames = ["xq", "wv", "tp", "vp", "sp"] + ["wq%d" % p for p in range(NPASS)]
    for s_ in semnames:
        P.sem(s_, es.enter_context(nc.semaphore(s_)))
    s_ld = es.enter_context(nc.semaphore("s_ld"))
    s_su = es.enter_context(nc.semaphore("s_su"))
    s_s2 = es.enter_context(nc.semaphore("s_s2"))
    s_e1 = es.enter_context(nc.semaphore("s_e1"))
    s_e2 = es.enter_context(nc.semaphore("s_e2"))

    SUMX, SUMSQ, MU, VAR, RSTD = 0, 4, 8, 12, 16
    LG, EX, MSK, T8 = 0, 32, 64, 96
    GM1, GM2, GSM, GRC, GCM = 128, 132, 136, 140, 144

    with nc.Block() as blk:
        @blk.sync
        def _(E):
            n = 0
            for k in dcon:
                E.dma_start(out=CON[k][:], in_=dcon[k]).then_inc(s_ld, 16)
                n += 1
            for dst, src in [(CtR, dctr), (CtI, dcti), (g_rep, dgr),
                             (be_rep, dbr), (bi_rep, dbi), (bg32, dbg),
                             (b1sb, db1)]:
                E.dma_start(out=dst[:], in_=src).then_inc(s_ld, 16)
                n += 1
            E.dma_start(out=POWR[:, 1:2], in_=dapr).then_inc(s_ld, 16)
            E.dma_start(out=POWI[:, 1:2], in_=dapi).then_inc(s_ld, 16)
            E.dma_start(out=ssc[:, 0:1], in_=dbpr).then_inc(s_ld, 16)
            E.dma_start(out=ssc[:, 1:2], in_=dbpi).then_inc(s_ld, 16)
            n += 4
            E.wait_ge(s_ld, 16 * (n + 10))

        @blk.gpsimd
        def _(E):
            for j in range(8):
                E.dma_start(out=wg[:, j * 8:(j + 1) * 8],
                            in_=dwg[j * 128:(j + 1) * 128, :]).then_inc(s_ld, 16)
            E.dma_start(out=b2sb[:], in_=db2).then_inc(s_ld, 16)
            E.dma_start(out=EYEB[:], in_=dcon["c_eyef"]).then_inc(s_ld, 16)

    with nc.Block() as blk:
        @blk.vector
        def _(E):
            E.memset(POWR[:, 0:1], 1.0)
            E.memset(POWI[:, 0:1], 0.0)
            E.tensor_copy(ssc[:, 2:3], POWR[:, 1:2])
            E.tensor_copy(ssc[:, 3:4], POWI[:, 1:2])
            E.drain()
            n = 1
            while n < 128:
                E.drain()
                E.tensor_scalar(PkR[:, 0:n], POWI[:, 0:n], ssc[:, 3:4], None, ALU.mult)
                E.drain()
                E.scalar_tensor_tensor(POWR[:, n:2 * n], POWR[:, 0:n], ssc[:, 2:3],
                                       PkR[:, 0:n], ALU.mult, ALU.subtract)
                E.tensor_scalar(PkR[:, 0:n], POWI[:, 0:n], ssc[:, 2:3], None, ALU.mult)
                E.drain()
                E.scalar_tensor_tensor(POWI[:, n:2 * n], POWR[:, 0:n], ssc[:, 3:4],
                                       PkR[:, 0:n], ALU.mult, ALU.add)
                E.tensor_tensor(ssc[:, 4:5], ssc[:, 2:3], ssc[:, 2:3], ALU.mult)
                E.tensor_tensor(ssc[:, 5:6], ssc[:, 3:4], ssc[:, 3:4], ALU.mult)
                E.tensor_tensor(ssc[:, 6:7], ssc[:, 2:3], ssc[:, 3:4], ALU.mult)
                E.drain()
                E.tensor_tensor(ssc[:, 2:3], ssc[:, 4:5], ssc[:, 5:6], ALU.subtract)
                E.tensor_scalar(ssc[:, 3:4], ssc[:, 6:7], 2.0, None, ALU.mult)
                E.drain()
                n *= 2
            E.drain()
            E.tensor_scalar(ssc[:, 4:5], POWI[:, 127:128], POWI[:, 1:2], None, ALU.mult)
            E.tensor_scalar(ssc[:, 5:6], POWI[:, 127:128], POWR[:, 1:2], None, ALU.mult)
            E.drain()
            E.scalar_tensor_tensor(POWR[:, 128:129], POWR[:, 127:128], POWR[:, 1:2],
                                   ssc[:, 4:5], ALU.mult, ALU.subtract)
            E.scalar_tensor_tensor(POWI[:, 128:129], POWR[:, 127:128], POWI[:, 1:2],
                                   ssc[:, 5:6], ALU.mult, ALU.add)
            E.drain()
            E.tensor_scalar(PkR[:, :], POWI[:, 0:128], ssc[:, 1:2], None, ALU.mult)
            E.drain()
            E.scalar_tensor_tensor(PkR[:, :], POWR[:, 0:128], ssc[:, 0:1],
                                   PkR[:, :], ALU.mult, ALU.subtract)
            E.tensor_scalar(PkI[:, :], POWI[:, 0:128], ssc[:, 0:1], None, ALU.mult)
            E.drain()
            E.scalar_tensor_tensor(PkI[:, :], POWR[:, 0:128], ssc[:, 1:2],
                                   PkI[:, :], ALU.mult, ALU.add)
            E.drain()
            E.tensor_scalar(GAnI[:, :], POWI[:, 1:129], -1.0, None, ALU.mult)
            E.memset(sre[:], 0.0)
            E.memset(sim[:], 0.0)
            E.memset(konst[:, 0:1], LN_EPS)
            E.memset(konst[:, 1:2], -0.5)

    for (src, dst, s_e) in ((PkR, ER, s_e1), (PkI, EI, s_e2)):
        with nc.Block() as blk:
            @blk.tensor
            def _(E, src=src):
                E.transpose(pb[0][:, 0:NSS], src[:, :],
                            CON["c_eyef"][0:NSS, 0:NSS]).then_inc(s_e, 1)

            @blk.vector
            def _(E, s_e=s_e):
                E.wait_ge(s_e, 1)
                E.tensor_copy(PTT[:], pb[0][:, 0:NSS]).then_inc(s_e, 1)

            @blk.tensor
            def _(E, s_e=s_e):
                E.wait_ge(s_e, 2)
                E.matmul(pb[1][:, 0:NSS], CON["c_rev"][:], PTT[:],
                         start=True, stop=True).then_inc(s_e, 1)

            @blk.vector
            def _(E, dst=dst, s_e=s_e):
                E.wait_ge(s_e, 3)
                E.tensor_copy(dst[:], pb[1][:, 0:NSS])

    with nc.Block() as blk:
        @blk.vector
        def _(E):
            E.tensor_scalar(PkI[:, :], PkI[:, :], -1.0, None, ALU.mult)

    with nc.Block() as blk:
        @blk.tensor
        def _(E):
            for h in range(2):
                E.matmul(pb[h][:], PkR[:, :], CtR[:, h * 512:(h + 1) * 512],
                         start=True, stop=False)
                E.matmul(pb[h][:], PkI[:, :], CtI[:, h * 512:(h + 1) * 512],
                         start=False, stop=True).then_inc(s_su, 1)

        @blk.vector
        def _(E):
            E.wait_ge(s_su, 2)
            for h in range(2):
                E.tensor_copy(ssm_in[:, h * 512:(h + 1) * 512], pb[h][:])

    with nc.Block() as blk:
        @blk.tensor
        def _(E):
            for h in range(2):
                E.matmul(pb[h][:], CON["c_kdf_r"][:],
                         ssm_in[:, h * 512:(h + 1) * 512], start=True, stop=True)
                E.matmul(pb[2 + h][:], CON["c_kdf_i"][:],
                         ssm_in[:, h * 512:(h + 1) * 512],
                         start=True, stop=True).then_inc(s_s2, 1)

        @blk.vector
        def _(E):
            E.wait_ge(s_s2, 2)
            for h in range(2):
                E.tensor_copy(Kfr[:, h * 512:(h + 1) * 512], pb[h][:])
                E.tensor_copy(Kfi[:, h * 512:(h + 1) * 512], pb[2 + h][:])

    decR = POWR[:, 128:129]
    decI = POWI[:, 128:129]
    GA_R = POWR[:, 1:129]

    state = {"pb_free": [None] * 8,
             "w1slot": 0, "w2slot": 0, "w1free": [None, None],
             "w2free": [None] * 4, "ltfree0": None, "ltfree1": None,
             "winfree": None,
             "lnT_ready": [0, 0], "comb_ready": [0, 0]}
    members_buf = {}

    def wait_mark(eng, mark):
        if mark is not None:
            P.wait(eng, mark[0], mark[1])

    def s_tile(buf, ts):
        return S[buf][:, ts * D:(ts + 1) * D]

    def emit_ln_stats_ts(src_sb, src_name, ts):
        xt = src_sb[:, ts * D:(ts + 1) * D]
        P.op("vector", lambda E, a=xt, c=ts: E.tensor_reduce(
            stats[:, SUMX + c:SUMX + c + 1], a, AXX, ALU.add),
            reads=(src_name,), writes=("stats",))
        P.op("vector", lambda E, a=xt, c=ts: E.scalar_tensor_tensor(
            sq[:], a, 1.0, a, ALU.mult, ALU.mult,
            accum_out=stats[:, SUMSQ + c:SUMSQ + c + 1]),
            reads=(src_name, "sq"), writes=("sq", "stats"))

    def emit_ln_stats_tail():
        P.op("vector", lambda E: E.tensor_scalar(
            stats[:, MU:MU + 4], stats[:, SUMX:SUMX + 4], 1.0 / D, None, ALU.mult),
            reads=("stats",), writes=("stats",))
        P.op("vector", lambda E: E.tensor_scalar(
            stats[:, VAR:VAR + 4], stats[:, SUMSQ:SUMSQ + 4], 1.0 / D, None, ALU.mult),
            reads=("stats",), writes=("stats",))
        P.op("vector", lambda E: E.tensor_tensor(
            lnrow[:, 0:4], stats[:, MU:MU + 4], stats[:, MU:MU + 4], ALU.mult),
            reads=("stats",), writes=("lnrow",))
        vv = P.op("vector", lambda E: E.tensor_tensor(
            stats[:, VAR:VAR + 4], stats[:, VAR:VAR + 4], lnrow[:, 0:4],
            ALU.subtract),
            reads=("stats", "lnrow"), writes=("stats",), inc="vp", fence=True)
        P.wait("scalar", "vp", vv)
        P.op("scalar", lambda E: E.activation(
            lnrow[:, 4:8], stats[:, VAR:VAR + 4], AF.Ln, bias=konst[:, 0:1]),
            reads=(), writes=("lnrow",))
        ss = P.op("scalar", lambda E: E.activation(
            stats[:, RSTD:RSTD + 4], lnrow[:, 4:8], AF.Exp,
            scale=konst[:, 1:2]),
            reads=("lnrow",), writes=("stats",), inc="sp", fence=True)
        P.wait("vector", "sp", ss)

    def emit_ln_stats(src_sb, src_name):
        for ts in range(NT):
            emit_ln_stats_ts(src_sb, src_name, ts)
        emit_ln_stats_tail()

    def emit_ln_t(src_sb, src_name, dst_sb, dst_name, skip_stats=False):
        if not skip_stats:
            emit_ln_stats(src_sb, src_name)
        dst3 = dst_sb[:].rearrange("p (j t) -> p j t", j=8)
        ltb = [lt32, tmpc]
        ltn = ["sq", "tmpc"]
        ltfree = [state["ltfree0"], state["ltfree1"]]
        tvs = [None] * NT

        def norm(ts):
            pi = ts % 2
            wait_mark("vector", ltfree[pi])
            nv = P.op("vector", lambda E, a=src_sb[:, ts * D:(ts + 1) * D],
                      c=ts, b=ltb[pi]: E.tensor_scalar(
                b[:], a, stats[:, MU + c:MU + c + 1],
                stats[:, RSTD + c:RSTD + c + 1], ALU.subtract, ALU.mult),
                reads=(src_name, "stats", ltn[pi], "tm0", "tm1"),
                writes=(ltn[pi],), inc="vp", fence=True)
            P.wait("tensor", "vp", nv)
            bks = (6, 7) if pi == 0 else (4, 5)
            tv = None
            for g in range(2):
                bk = bks[g]
                wait_mark("tensor", state["pb_free"][bk])
                for q in range(4):
                    dj = g * 4 + q
                    last = (g == 1 and q == 3)
                    tv2 = P.op("tensor",
                        lambda E, b=bk, qq=q, a=dj, lb=ltb[pi]: E.transpose(
                            pb[b][:, qq * 128:(qq + 1) * 128],
                            lb[:, a * 128:(a + 1) * 128], CON["c_eyef"][:]),
                        inc="tp" if last else None)
                    if last:
                        tv = tv2
            ltfree[pi] = ("tp", tv)
            tvs[ts] = tv

        def copies(ts):
            pi = ts % 2
            bks = (6, 7) if pi == 0 else (4, 5)
            P.wait("vector", "tp", tvs[ts])
            for g in range(2):
                bk = bks[g]
                dv = P.op("vector", lambda E, b=bk, gg=g, c=ts: E.tensor_copy(
                    dst3[:, gg * 4:(gg + 1) * 4, c * 128:(c + 1) * 128],
                    pb[b][:].rearrange("p (j t) -> p j t", j=4)),
                    reads=(), writes=(dst_name,), inc="vp", fence=True)
                state["pb_free"][bk] = ("vp", dv)

        norm(0)
        for ts in range(1, NT):
            norm(ts)
            copies(ts - 1)
        copies(NT - 1)
        state["ltfree0"], state["ltfree1"] = ltfree
        return P.cnt["vp"]

    def emit_gates(lnT_sb, comb_sb, comb_name, lnT_ready):
        P.wait("tensor", "vp", lnT_ready)
        wait_mark("tensor", state["pb_free"][4])
        tv = None
        for ts in range(NT):
            for dj in range(8):
                last = (ts == NT - 1 and dj == 7)
                tv2 = P.op("tensor", lambda E, c=ts, a=dj: E.matmul(
                    pb[4][:, c * 8:(c + 1) * 8],
                    lnT_sb[:, a * 512 + c * 128:a * 512 + c * 128 + 128],
                    wg[:, a * 8:(a + 1) * 8],
                    start=(a == 0), stop=(a == 7)),
                    inc="tp" if last else None)
                if last:
                    tv = tv2
        P.wait("vector", "tp", tv)
        P.op("vector", lambda E: E.tensor_tensor(
            gsc[:, LG:LG + 32], pb[4][:, 0:32], bg32[:], ALU.add),
            reads=(), writes=("gsc",))
        state["pb_free"][4] = ("vp", P.fence("vector", "vp"))
        lg3 = gsc[:, LG:LG + 32].rearrange("p (t e) -> p t e", e=8)
        t83 = gsc[:, T8:T8 + 32].rearrange("p (t e) -> p t e", e=8)
        ex3 = gsc[:, EX:EX + 32].rearrange("p (t e) -> p t e", e=8)
        gm1_3 = gsc[:, GM1:GM1 + 4].rearrange("p (t o) -> p t o", o=1)
        gm2_3 = gsc[:, GM2:GM2 + 4].rearrange("p (t o) -> p t o", o=1)
        gsm_3 = gsc[:, GSM:GSM + 4].rearrange("p (t o) -> p t o", o=1)
        P.op("vector", lambda E: E.tensor_reduce(gm1_3, lg3, AXX, ALU.max),
             reads=("gsc",), writes=("gsc",))
        for ts in range(NT):
            P.op("vector", lambda E, c=ts: E.tensor_scalar(
                gsc[:, MSK + c * 8:MSK + (c + 1) * 8],
                gsc[:, LG + c * 8:LG + (c + 1) * 8],
                gsc[:, GM1 + c:GM1 + c + 1], None, ALU.is_ge),
                reads=("gsc",), writes=("gsc",))
        for ts in range(NT):
            P.op("vector", lambda E, c=ts: E.scalar_tensor_tensor(
                gsc[:, T8 + c * 8:T8 + (c + 1) * 8],
                gsc[:, MSK + c * 8:MSK + (c + 1) * 8], -1e30,
                gsc[:, LG + c * 8:LG + (c + 1) * 8], ALU.mult, ALU.add),
                reads=("gsc",), writes=("gsc",))
        P.op("vector", lambda E: E.tensor_reduce(gm2_3, t83, AXX, ALU.max),
             reads=("gsc",), writes=("gsc",))
        for ts in range(NT):
            P.op("vector", lambda E, c=ts: E.tensor_scalar(
                gsc[:, T8 + c * 8:T8 + (c + 1) * 8],
                gsc[:, T8 + c * 8:T8 + (c + 1) * 8],
                gsc[:, GM2 + c:GM2 + c + 1], None, ALU.is_ge),
                reads=("gsc",), writes=("gsc",))
        P.op("vector", lambda E: E.tensor_tensor(
            gsc[:, MSK:MSK + 32], gsc[:, MSK:MSK + 32], gsc[:, T8:T8 + 32],
            ALU.add),
            reads=("gsc",), writes=("gsc",))
        vv = P.fence("vector", "vp")
        P.wait("scalar", "vp", vv)
        ss = P.op("scalar", lambda E: E.activation(
            gsc[:, EX:EX + 32], gsc[:, LG:LG + 32], AF.Exp),
            reads=(), writes=("gsc",), inc="sp", fence=True)
        P.wait("vector", "sp", ss)
        P.op("vector", lambda E: E.tensor_reduce(gsm_3, ex3, AXX, ALU.add),
             reads=("gsc",), writes=("gsc",))
        P.op("vector", lambda E: E.reciprocal(
            gsc[:, GRC:GRC + 4], gsc[:, GSM:GSM + 4]),
            reads=("gsc",), writes=("gsc",))
        for ts in range(NT):
            P.op("vector", lambda E, c=ts: E.tensor_tensor(
                gsc[:, GCM:GCM + 8], gsc[:, EX + c * 8:EX + (c + 1) * 8],
                gsc[:, MSK + c * 8:MSK + (c + 1) * 8], ALU.mult),
                reads=("gsc",), writes=("gsc",))
            P.op("vector", lambda E, c=ts: E.tensor_scalar(
                comb_sb[:, c * 8:(c + 1) * 8], gsc[:, GCM:GCM + 8],
                gsc[:, GRC + c:GRC + c + 1], 0.5, ALU.mult, ALU.mult),
                reads=("gsc",), writes=(comb_name,))
        return P.fence("vector", "vp")

    def emit_b2(comb_sb, comb_name, dst_buf, comb_ready):
        P.wait("tensor", "vp", comb_ready)
        wait_mark("tensor", state["pb_free"][5])
        tv = None
        for ts in range(NT):
            last = (ts == NT - 1)
            tv2 = P.op("tensor", lambda E, c=ts: E.transpose(
                pb[5][0:8, c * 128:(c + 1) * 128],
                comb_sb[:, c * 8:(c + 1) * 8], CON["c_eyef"][:]),
                inc="tp" if last else None)
            if last:
                tv = tv2
        P.wait("vector", "tp", tv)
        cv = P.op("vector", lambda E: E.tensor_copy(combT[:], pb[5][0:8, 0:512]),
                  reads=(), writes=("combT",), inc="vp", fence=True)
        state["pb_free"][5] = ("vp", cv)
        P.wait("tensor", "vp", cv)
        for ts in range(NT):
            for dh in range(2):
                bk = 4 + dh
                wait_mark("tensor", state["pb_free"][bk])
                tv2 = P.op("tensor", lambda E, c=ts, d=dh, b=bk: E.matmul(
                    pb[b][:], combT[:, c * 128:(c + 1) * 128],
                    b2sb[:, d * 512:(d + 1) * 512], start=True, stop=True),
                    inc="tp" if dh == 1 else None)
                if dh == 1:
                    tv = tv2
            P.wait("vector", "tp", tv)
            for dh in range(2):
                bk = 4 + dh
                dst = S[dst_buf][:, ts * D + dh * 512:ts * D + dh * 512 + 512]
                state["pb_free"][bk] = ("vp", P.op(
                    "vector", lambda E, o=dst, b=bk: E.tensor_tensor(
                        o, o, pb[b][:], ALU.add),
                    reads=("s%d" % dst_buf,), writes=("s%d" % dst_buf,),
                    inc="vp", fence=True))

    W1BK = {0: (0, 1, 4), 1: (2, 3, 5)}

    def emit_pass(p, members):
        wq = "wq%d" % p
        for (mi, ck) in members:
            P.wait("tensor", "vp", state["lnT_ready"][mi])
        for e in range(NE):
            for half in range(2):
                for w in range(4):
                    slot = state["w1slot"] % 2
                    state["w1slot"] += 1
                    wait_mark("gpsimd", state["w1free"][slot])
                    for dj in range(8):
                        P.op("gpsimd", lambda E, a=e, b=half, c=w, d=dj, s=slot:
                             E.dma_start(
                                 out=w1r[:, s * 4096 + d * 512:
                                         s * 4096 + d * 512 + 512],
                                 in_=dw1[a * D + d * 128:a * D + (d + 1) * 128,
                                         b * 2048 + c * 512:
                                         b * 2048 + (c + 1) * 512]),
                             inc=wq, n=16)
                    P.wait("tensor", wq, P.cnt[wq])
                    for k in range(4):
                        ht = w * 4 + k
                        for (mi, ck) in members:
                            bk = W1BK[mi][ht % 3]
                            wait_mark("tensor", state["pb_free"][bk])
                        for dj in range(8):
                            for wi, (mi, ck) in enumerate(members):
                                bk = W1BK[mi][ht % 3]
                                lastmm = (dj == 7 and wi == len(members) - 1)
                                tv2 = P.op("tensor",
                                     lambda E, a=dj, b=bk, m=mi, s=slot, kk=k:
                                     E.matmul(
                                         pb[b][:],
                                         w1r[:, s * 4096 + a * 512 + kk * 128:
                                             s * 4096 + a * 512 + kk * 128 + 128],
                                         lnT[m][:, a * 512:(a + 1) * 512],
                                         start=(a == 0), stop=(a == 7)),
                                     inc="tp" if lastmm else None)
                                if lastmm:
                                    tv = tv2
                        P.wait("scalar", "tp", tv)
                        for (mi, ck) in members:
                            bk = W1BK[mi][ht % 3]
                            state["pb_free"][bk] = ("sp", P.op(
                                "scalar",
                                lambda E, b=bk, m=mi, hh=ht, a=e, hf=half:
                                E.activation(
                                    hbuf[m][:, hh * 512:(hh + 1) * 512],
                                    pb[b][:], AF.Gelu,
                                    bias=b1sb[:, a * 32 + hf * 16 + hh:
                                              a * 32 + hf * 16 + hh + 1]),
                                reads=(), writes=("h_%d" % mi,),
                                inc="sp", fence=True))
                    state["w1free"][slot] = ("tp", tv)
                gelu_done = P.cnt["sp"]
                for dh in range(2):
                    hg_wq = [None] * 4
                    for hg in range(4):
                        slot = state["w2slot"] % 4
                        state["w2slot"] += 1
                        wait_mark("gpsimd", state["w2free"][slot])
                        for d_ in range(4):
                            ht = hg * 4 + d_
                            P.op("gpsimd",
                                 lambda E, a=e, b=half, c=dh, d=ht, s=slot:
                                 E.dma_start(
                                     out=w2r[:, s * 2048 + (d % 4) * 512:
                                             s * 2048 + (d % 4) * 512 + 512],
                                     in_=dw2[a * HID + b * 2048 + d * 128:
                                             a * HID + b * 2048 + (d + 1) * 128,
                                             c * 512:(c + 1) * 512]),
                                 inc=wq, n=16)
                        hg_wq[hg] = P.cnt[wq]
                        state["hg_slot%d" % hg] = slot
                    P.wait("tensor", "sp", gelu_done)
                    nmem = len(members)
                    lag = W2LAG if nmem == 2 else 0
                    nsteps = 16 + lag
                    for step in range(nsteps):
                        work = []
                        if step < 16:
                            work.append((0, step))
                        if nmem == 2 and 0 <= step - lag < 16:
                            work.append((1, step - lag))
                        evac_req = []
                        for wi, (midx, ht) in enumerate(work):
                            mi, ck = members[midx]
                            if ht % 4 == 0:
                                P.wait("tensor", wq, hg_wq[ht // 4])
                            slot = state["hg_slot%d" % (ht // 4)]
                            base = [0, 4][midx] if nmem == 2 else 0
                            for ts in range(NT):
                                bk = base + ts
                                if ht == 0:
                                    wait_mark("tensor", state["pb_free"][bk])
                                lastmm = (wi == len(work) - 1 and ts == NT - 1)
                                tv2 = P.op("tensor",
                                     lambda E, b=bk, m=mi, d=ht, c=ts, s=slot:
                                     E.matmul(
                                         pb[b][:],
                                         hbuf[m][:, d * 512 + c * 128:
                                                 d * 512 + c * 128 + 128],
                                         w2r[:, s * 2048 + (d % 4) * 512:
                                             s * 2048 + (d % 4) * 512 + 512],
                                         start=(d == 0), stop=(d == 15)),
                                     inc="tp" if lastmm else None)
                                if lastmm:
                                    tv = tv2
                            if ht == 15:
                                evac_req.append((midx, base))
                        if step - lag in (3, 7, 11) and step >= lag:
                            state["w2free"][state["hg_slot%d" % ((step - lag) // 4)]] = ("tp", tv)
                        for (midx, base) in evac_req:
                            mi, ck = members[midx]
                            P.wait("vector", "tp", tv)
                            sbuf_i = members_buf[mi]
                            for ts in range(NT):
                                bk = base + ts
                                dst = S[sbuf_i][:, ts * D + dh * 512:
                                                ts * D + dh * 512 + 512]
                                state["pb_free"][bk] = ("vp", P.op(
                                    "vector",
                                    lambda E, o=dst, b=bk, m=mi, c=ts, a=e:
                                    E.scalar_tensor_tensor(
                                        o, pb[b][:],
                                        comb[m][:, c * 8 + a:c * 8 + a + 1],
                                        o, ALU.mult, ALU.add),
                                    reads=("s%d" % sbuf_i,),
                                    writes=("s%d" % sbuf_i,),
                                    inc="vp", fence=True))
                                if (mi == 1 and e == NE - 1 and half == 1
                                        and dh == 1):
                                    emit_ln_stats_ts(S[sbuf_i],
                                                     "s%d" % sbuf_i, ts)
                                    if ts == 0:
                                        emit_ln_stats_tail()
                    state["w2free"][state["hg_slot3"]] = ("tp", tv)

    def emit_tprod():
        P.op("vector", lambda E: E.tensor_tensor(
            sq[0:NSS, :], CtI[:], sim[:], ALU.mult),
            reads=("sim0", "sq"), writes=("sq",))
        P.op("vector", lambda E: E.tensor_tensor(
            TR[:], CtR[:], sre[:], ALU.mult),
            reads=("sre0",), writes=("TRb",))
        P.op("vector", lambda E: E.tensor_tensor(
            TR[:], TR[:], sq[0:NSS, :], ALU.subtract),
            reads=("TRb", "sq"), writes=("TRb",))
        P.op("vector", lambda E: E.tensor_tensor(
            tmpc[0:NSS, :], CtI[:], sre[:], ALU.mult),
            reads=("sre0", "tmpc", "tm0", "tm1"), writes=("tmpc",))
        P.op("vector", lambda E: E.tensor_tensor(
            TI[:], CtR[:], sim[:], ALU.mult),
            reads=("sim0",), writes=("TIb",))
        P.op("vector", lambda E: E.tensor_tensor(
            TI[:], TI[:], tmpc[0:NSS, :], ALU.add),
            reads=("TIb", "tmpc"), writes=("TIb",))
        return P.fence("vector", "vp")

    def emit_ssm(c):
        buf = c % 2
        if c == SEQCH:
            P.op("vector", lambda E: E.memset(sre[:], 0.0), writes=("sre0",))
            P.op("vector", lambda E: E.memset(sim[:], 0.0), writes=("sim0",))
        tprod0 = emit_tprod()
        ln2v = emit_ln_t(S[buf], "s%d" % buf, lnT[1], "lnT_b",
                         skip_stats=True)
        P.wait("tensor", "vp", ln2v)
        for dh in range(2):
            if dh == 0:
                wait_mark("gpsimd", state["winfree"])
            for j in range(8):
                wv = P.op("gpsimd", lambda E, jj=j, d=dh: E.dma_start(
                    out=winT[:, jj * 512:(jj + 1) * 512],
                    in_=dwin[jj * 128:(jj + 1) * 128, d * 512:(d + 1) * 512]),
                    inc="wv", n=16)
            P.wait("tensor", "wv", wv)
            for ts in range(NT):
                bk = ts % 2
                wait_mark("tensor", state["pb_free"][bk])
                for dj in range(8):
                    tv2 = P.op("tensor", lambda E, a=dj, b=bk, c2=ts: E.matmul(
                        pb[b][:],
                        lnT[1][:, a * 512 + c2 * 128:a * 512 + c2 * 128 + 128],
                        winT[:, a * 512:(a + 1) * 512],
                        start=(a == 0), stop=(a == 7)),
                        inc="tp" if dj == 7 else None)
                    if dj == 7:
                        tv = tv2
                P.wait("vector", "tp", tv)
                state["pb_free"][bk] = ("vp", P.op(
                    "vector", lambda E, c2=ts, d=dh, b=bk: E.tensor_tensor(
                        ssm_in[:, c2 * D + d * 512:c2 * D + d * 512 + 512],
                        pb[b][:], bi_rep[:, d * 512:(d + 1) * 512], ALU.add),
                    reads=(), writes=("ssm_in",), inc="vp", fence=True))
                if dh == 1:
                    emit_ln_stats_ts(ssm_in, "ssm_in", ts)
            if dh == 0:
                P.wait("gpsimd", "tp", tv)
            else:
                state["winfree"] = ("tp", tv)
        emit_ln_stats_tail()
        for ts in range(NT):
            xt = ssm_in[:, ts * D:(ts + 1) * D]
            P.op("vector", lambda E, a=xt, c2=ts: E.tensor_scalar(
                tmpc[:], a, stats[:, MU + c2:MU + c2 + 1],
                stats[:, RSTD + c2:RSTD + c2 + 1], ALU.subtract, ALU.mult),
                reads=("ssm_in", "stats"), writes=("tmpc",))
            P.op("vector", lambda E: E.tensor_tensor(
                tmpc[:], tmpc[:], g_rep[:], ALU.mult),
                reads=("tmpc",), writes=("tmpc",))
            P.op("vector", lambda E, a=xt: E.tensor_tensor(
                a, tmpc[:], be_rep[:], ALU.add),
                reads=("tmpc",), writes=("ssm_in",))
        ssm_ready = P.fence("vector", "vp")
        P.wait("tensor", "vp", ssm_ready)

        def contrib_mms(ts2):
            xin2 = ssm_in[:, ts2 * D:(ts2 + 1) * D]
            for bk in range(4):
                wait_mark("tensor", state["pb_free"][bk])
            cd = None
            for h2 in range(2):
                P.op("tensor", lambda E, h=h2, a=xin2: E.matmul(
                    pb[h][0:NSS, :], ER[:], a[:, h * 512:(h + 1) * 512],
                    start=True, stop=True))
                cd = P.op("tensor", lambda E, h=h2, a=xin2: E.matmul(
                    pb[2 + h][0:NSS, :], EI[:], a[:, h * 512:(h + 1) * 512],
                    start=True, stop=True),
                    inc="tp" if h2 == 1 else None)
            return cd

        ctr_done = contrib_mms(0)
        tprod = tprod0
        for ts in range(NT):
            xin_t = ssm_in[:, ts * D:(ts + 1) * D]
            P.wait("tensor", "vp", tprod)
            for bk in (4, 5):
                wait_mark("tensor", state["pb_free"][bk])
            far_done = None
            for h2 in range(2):
                P.op("tensor", lambda E, h=h2: E.matmul(
                    pb[4 + h][:], GA_R, TR[:, h * 512:(h + 1) * 512],
                    start=True, stop=False))
                far_done = P.op("tensor", lambda E, h=h2: E.matmul(
                    pb[4 + h][:], GAnI[:], TI[:, h * 512:(h + 1) * 512],
                    start=False, stop=False),
                    inc="tp" if h2 == 1 else None)
            P.wait("vector", "tp", ctr_done)
            P.op("vector", lambda E: E.tensor_scalar(
                sq[0:NSS, :], sim[:], decI, None, ALU.mult),
                reads=("sim0", "sq"), writes=("sq",))
            P.op("vector", lambda E: E.tensor_scalar(
                tmpc[0:NSS, :], sre[:], decI, None, ALU.mult),
                reads=("sre0", "tmpc", "tm0", "tm1"), writes=("tmpc",))
            P.op("vector", lambda E: E.scalar_tensor_tensor(
                sre[:], sre[:], decR, sq[0:NSS, :], ALU.mult, ALU.subtract),
                reads=("sre0", "sq"), writes=("sre0",))
            for h2 in range(2):
                P.op("vector", lambda E, h=h2: E.tensor_tensor(
                    sre[:, h * 512:(h + 1) * 512],
                    sre[:, h * 512:(h + 1) * 512], pb[h][0:NSS, :], ALU.add),
                    reads=("sre0",), writes=("sre0",))
            P.op("vector", lambda E: E.scalar_tensor_tensor(
                sim[:], sim[:], decR, tmpc[0:NSS, :], ALU.mult, ALU.add),
                reads=("sim0", "tmpc"), writes=("sim0",))
            for h2 in range(2):
                P.op("vector", lambda E, h=h2: E.tensor_tensor(
                    sim[:, h * 512:(h + 1) * 512],
                    sim[:, h * 512:(h + 1) * 512], pb[2 + h][0:NSS, :],
                    ALU.add),
                    reads=("sim0",), writes=("sim0",))
            st_done = P.fence("vector", "vp")
            if ts < NT - 1:
                P.wait("vector", "tp", far_done)
                tprod = emit_tprod()
            if ts < NT - 1:
                P.wait("tensor", "vp", st_done)
                ctr_done = contrib_mms(ts + 1)
            y_done = None
            for h2 in range(2):
                sl = slice(h2 * 512, (h2 + 1) * 512)
                for bk in (6, 7):
                    wait_mark("tensor", state["pb_free"][bk])
                P.op("tensor", lambda E, a=xin_t, s_=sl: E.matmul(
                    pb[6][:], CON["c_fwd_r"][:], a[:, s_],
                    start=True, stop=True))
                xf_done = P.op("tensor", lambda E, a=xin_t, s_=sl: E.matmul(
                    pb[7][:], CON["c_fwd_i"][:], a[:, s_],
                    start=True, stop=True), inc="tp")
                P.wait("vector", "tp", xf_done)
                P.op("vector", lambda E, s_=sl: E.tensor_tensor(
                    Yfr[:, 0:512], pb[6][:], Kfr[:, s_], ALU.mult),
                    reads=("Yf0",), writes=("Yf0",))
                P.op("vector", lambda E, s_=sl: E.tensor_tensor(
                    tmpc[:, 0:512], pb[7][:], Kfi[:, s_], ALU.mult),
                    reads=("tm0",), writes=("tm0",))
                P.op("vector", lambda E, s_=sl: E.tensor_tensor(
                    Yfr[:, 512:1024], pb[6][:], Kfi[:, s_], ALU.mult),
                    reads=("Yf1",), writes=("Yf1",))
                P.op("vector", lambda E, s_=sl: E.tensor_tensor(
                    tmpc[:, 512:1024], pb[7][:], Kfr[:, s_], ALU.mult),
                    reads=("tm1",), writes=("tm1",))
                P.op("vector", lambda E: E.tensor_tensor(
                    Yfr[:, 0:512], Yfr[:, 0:512], tmpc[:, 0:512], ALU.subtract),
                    reads=("Yf0", "tm0"), writes=("Yf0",))
                P.op("vector", lambda E: E.tensor_tensor(
                    Yfr[:, 512:1024], Yfr[:, 512:1024], tmpc[:, 512:1024],
                    ALU.add),
                    reads=("Yf1", "tm1"), writes=("Yf1",))
                yfh = P.fence("vector", "vp")
                state["pb_free"][6] = ("vp", yfh)
                state["pb_free"][7] = ("vp", yfh)
                P.wait("tensor", "vp", yfh)
                P.op("tensor", lambda E, h=h2: E.matmul(
                    pb[4 + h][:], CON["c_inv_r"][:],
                    Yfr[:, 0:512], start=False, stop=False))
                y_done = P.op("tensor", lambda E, h=h2: E.matmul(
                    pb[4 + h][:], CON["c_inv_i"][:],
                    Yfr[:, 512:1024], start=False, stop=True),
                    inc="tp")
            P.wait("vector", "tp", y_done)
            for h2 in range(2):
                dst = S[buf][:, ts * D + h2 * 512:ts * D + h2 * 512 + 512]
                P.op("vector", lambda E, h=h2, a=xin_t: E.tensor_tensor(
                    tmpc[:, h * 512:(h + 1) * 512], pb[4 + h][:],
                    a[:, h * 512:(h + 1) * 512], ALU.add),
                    reads=("ssm_in", "tmpc", "tm0", "tm1"), writes=("tmpc",)),
                P.op("vector", lambda E, o=dst, h=h2: E.tensor_tensor(
                    o, o, tmpc[:, h * 512:(h + 1) * 512], ALU.add),
                    reads=("tmpc", "s%d" % buf), writes=("s%d" % buf,))
            emit_ln_stats_ts(S[buf], "s%d" % buf, ts)
            if ts == NT - 1:
                emit_ln_stats_tail()
            ev = P.fence("vector", "vp")
            state["pb_free"][4] = ("vp", ev)
            state["pb_free"][5] = ("vp", ev)

    xv = dx.rearrange("(c t p) d -> c p t d", t=NT, p=128)
    yv = dy.rearrange("(c t p) d -> c p t d", t=NT, p=128)

    def emit_load(c):
        sv = S[c % 2][:].rearrange("p (t d) -> p t d", t=NT)
        return P.op("sync", lambda E, cc=c, o=sv: E.dma_start(out=o, in_=xv[cc]),
                    inc="xq", n=16)

    def emit_store(c, v_ready):
        P.wait("sync", "vp", v_ready)
        sv = S[c % 2][:].rearrange("p (t d) -> p t d", t=NT)
        return P.op("sync", lambda E, cc=c, i=sv: E.dma_start(out=yv[cc], in_=i),
                    inc="xq", n=16)

    stage_n = [0]

    class StopSched(Exception):
        pass

    def tick():
        stage_n[0] += 1
        if stop_stage is not None and stage_n[0] >= stop_stage:
            raise StopSched()

    l0 = emit_load(0)
    l1 = emit_load(1)
    P.wait("vector", "xq", l0)
    state["lnT_ready"][1] = emit_ln_t(S[0], "s0", lnT[1], "lnT_b")
    cv = emit_gates(lnT[1], comb[1], "comb_b", state["lnT_ready"][1])
    emit_b2(comb[1], "comb_b", 0, cv)

    store_after = {}
    try:
      tick()
      for p in range(NPASS):
        members = []
        if p >= 1:
            members.append((0, p - 1))
        if p <= NCH - 1:
            members.append((1, p))
        members_buf.clear()
        for (mi, ck) in members:
            members_buf[mi] = ck % 2
        emit_pass(p, members)
        tick()
        if p >= 1:
            store_after[p - 1] = P.cnt["vp"]
        if p <= NCH - 1:
            c = p
            if p >= 1:
                stv = emit_store(p - 1, store_after[p - 1])
                if p + 1 <= NCH - 1:
                    P.wait("sync", "xq", stv)
                    emit_load(p + 1)
            emit_ssm(c)
            tick()
            state["lnT_ready"][0] = emit_ln_t(
                S[c % 2], "s%d" % (c % 2), lnT[0], "lnT_a", skip_stats=True)
            cva = emit_gates(lnT[0], comb[0], "comb_a", state["lnT_ready"][0])
            emit_b2(comb[0], "comb_a", c % 2, cva)
            tick()
            if p + 1 <= NCH - 1:
                P.wait("vector", "xq", P.cnt["xq"])
                state["lnT_ready"][1] = emit_ln_t(
                    S[(p + 1) % 2], "s%d" % ((p + 1) % 2), lnT[1], "lnT_b")
                cvb = emit_gates(lnT[1], comb[1], "comb_b",
                                 state["lnT_ready"][1])
                emit_b2(comb[1], "comb_b", (p + 1) % 2, cvb)
                tick()
      emit_store(NCH - 1, store_after[NCH - 1])
    except StopSched:
        fv = P.fence("vector", "vp")
        emit_store(0, fv)
    P.wait("sync", "xq", P.cnt["xq"])

    with nc.Block() as blk:
        @blk.tensor
        def _(E):
            for f in P.ops["tensor"]:
                f(E)

        @blk.vector
        def _(E):
            for f in P.ops["vector"]:
                f(E)

        @blk.scalar
        def _(E):
            for f in P.ops["scalar"]:
                f(E)

        @blk.sync
        def _(E):
            for f in P.ops["sync"]:
                f(E)

        @blk.gpsimd
        def _(E):
            for f in P.ops["gpsimd"]:
                f(E)

    es.close()
    return nc


_NC_CACHE = None


def kernel(L0_attn_x, L0_attn_W_in, L0_attn_b_in, L0_attn_gamma, L0_attn_beta,
           L0_attn_A_re, L0_attn_A_im, L0_attn_B_re, L0_attn_B_im,
           L0_attn_C_re, L0_attn_C_im, L0_attn_D_re, L0_attn_D_im,
           L0_ffn_Wg, L0_ffn_bg, L0_ffn_W1, L0_ffn_b1, L0_ffn_W2, L0_ffn_b2):
    global _NC_CACHE
    x = np.asarray(L0_attn_x, dtype=np.float32)
    B, L, _ = x.shape
    shared = host_consts()
    shared["w1"] = np.ascontiguousarray(
        np.asarray(L0_ffn_W1, np.float32).reshape(NE * D, HID))
    shared["w2"] = np.ascontiguousarray(
        np.asarray(L0_ffn_W2, np.float32).reshape(NE * HID, D))
    shared["wgd"] = np.ascontiguousarray(np.asarray(L0_ffn_Wg, np.float32))
    shared["b2d"] = np.ascontiguousarray(np.asarray(L0_ffn_b2, np.float32))
    b1 = np.asarray(L0_ffn_b1, np.float32)
    shared["b1d"] = np.ascontiguousarray(
        b1.reshape(NE, 32, 128).transpose(2, 0, 1).reshape(128, 256))
    shared["winTd"] = np.ascontiguousarray(np.asarray(L0_attn_W_in, np.float32).T)
    shared["g_repd"] = np.ascontiguousarray(np.broadcast_to(
        np.asarray(L0_attn_gamma, np.float32)[None, :], (128, D)))
    shared["be_repd"] = np.ascontiguousarray(np.broadcast_to(
        np.asarray(L0_attn_beta, np.float32)[None, :], (128, D)))
    shared["bi_repd"] = np.ascontiguousarray(np.broadcast_to(
        np.asarray(L0_attn_b_in, np.float32)[None, :], (128, D)))
    shared["bg32d"] = np.ascontiguousarray(np.broadcast_to(
        np.tile(np.asarray(L0_ffn_bg, np.float32), 4)[None, :], (128, 32)))
    shared["ct_red"] = np.ascontiguousarray(np.concatenate(
        [np.asarray(L0_attn_C_re, np.float32).T,
         np.asarray(L0_attn_D_re, np.float32)[None, :]], axis=0))
    shared["ct_imd"] = np.ascontiguousarray(np.concatenate(
        [np.asarray(L0_attn_C_im, np.float32).T,
         np.zeros((1, D), np.float32)], axis=0))
    shared["apad_red"] = np.ascontiguousarray(np.concatenate(
        [np.asarray(L0_attn_A_re, np.float32), [np.float32(1.0)]])[:, None])
    shared["apad_imd"] = np.ascontiguousarray(np.concatenate(
        [np.asarray(L0_attn_A_im, np.float32), [np.float32(0.0)]])[:, None])
    shared["bpad_red"] = np.ascontiguousarray(np.concatenate(
        [np.asarray(L0_attn_B_re, np.float32), [np.float32(1.0)]])[:, None])
    shared["bpad_imd"] = np.ascontiguousarray(np.concatenate(
        [np.asarray(L0_attn_B_im, np.float32), [np.float32(0.0)]])[:, None])

    if _NC_CACHE is None:
        _NC_CACHE = build_program()
    nc = _NC_CACHE

    in_maps = []
    for i in range(8):
        m = dict(shared)
        m["xin"] = np.ascontiguousarray(x[2 * i:2 * i + 2].reshape(NCH * CH, D))
        in_maps.append(m)

    res = run_bass_kernel_spmd(nc, in_maps, core_ids=list(range(8)))
    out = np.stack([np.asarray(r["yout"]).reshape(2, L, D) for r in res.results])
    return out.reshape(B, L, D).astype(np.float32)
